# version 32
# baseline (speedup 1.0000x reference)
import numpy as np
import ml_dtypes

import concourse.bass as bass
import concourse.mybir as mybir
from concourse.alu_op_type import AluOpType as OP
from concourse.bass_utils import run_bass_kernel_spmd

F32 = mybir.dt.float32
BF16 = mybir.dt.bfloat16
AF = mybir.ActivationFunctionType
BF = ml_dtypes.bfloat16

D = 2048
L = 2048
NDC = 16
NI = 64
ENGS = ("sync", "gpsimd", "tensor", "scalar", "vector")

DEBUG_TAPS = False


class Ctr:
    def __init__(self, sem):
        self.sem = sem
        self.n = 0


class Prog:

    def __init__(self):
        self.ops = {k: [] for k in ENGS}
        self.marks = {}
        self.trace = {k: [] for k in ENGS}

    def w(self, eng, ctr, key=None, val=None, opt=False):
        sem = ctr.sem
        self.trace[eng].append(("wait", id(ctr), key, val, opt))
        if key is not None:
            if opt:
                self.ops[eng].append(
                    lambda e: e.wait_ge(sem, self.marks[key]) if key in self.marks else None)
            else:
                self.ops[eng].append(lambda e: e.wait_ge(sem, self.marks[key]))
        else:
            self.ops[eng].append(lambda e: e.wait_ge(sem, val))

    def do(self, eng, fn, ctr=None, inc=1, mark=None):
        if ctr is not None:
            ctr.n += inc
            if mark is not None:
                self.marks[mark] = ctr.n
            self.trace[eng].append(("inc", id(ctr), inc, mark))
            self.ops[eng].append(lambda e: fn(e).then_inc(ctr.sem, inc))
        else:
            self.trace[eng].append(("op", None, 0, None))
            self.ops[eng].append(fn)

    def dma(self, eng, fn, ctr, mark=None):
        self.do(eng, fn, ctr, 16, mark)

    def mk(self, key, ctr):
        self.marks[key] = ctr.n


def _hadamard(n):
    i = np.arange(n)
    a = i[:, None] & i[None, :]
    par = np.zeros((n, n), np.int64)
    while a.any():
        par ^= a & 1
        a >>= 1
    return (1.0 - 2.0 * par).astype(np.float32)


def host_consts():
    c = {}
    H16 = _hadamard(16)
    H64 = _hadamard(64)
    c["h128f"] = _hadamard(128).astype(np.float32)
    c["h128b"] = _hadamard(128).astype(BF)
    p = np.arange(128)
    c["s1b"] = ((p[:, None] >> 4 == p[None, :] >> 4) * H16[np.ix_(p & 15, p & 15)]).astype(BF)
    c["s2b"] = ((p[:, None] >> 6 == p[None, :] >> 6) * H64[np.ix_(p & 63, p & 63)]).astype(BF)
    Rp = np.zeros((16, 128, 128), np.float32)
    Sj = np.zeros((16, 128, 128), np.float32)
    for j in range(16):
        Rp[j, 8 * j + (p >> 4), p] = 1.0
        Sj[j, p, 8 * j + (p >> 4)] = 1.0
    c["rpb"] = np.ascontiguousarray(Rp.transpose(1, 0, 2).reshape(128, 2048)).astype(BF)
    c["sjb"] = np.ascontiguousarray(Sj.transpose(1, 0, 2).reshape(128, 2048)).astype(BF)
    REPB = np.zeros((32, 128), np.float32)
    REPC = np.zeros((32, 128), np.float32)
    REPB[p & 15, p] = 1.0
    REPC[16 + (p & 15), p] = 1.0
    c["repbb"] = REPB.astype(BF)
    c["repcb"] = REPC.astype(BF)
    c["ones128b"] = np.ones((128, 1), BF)
    c["onesk1f"] = np.ones((1, 128), np.float32)
    return c


def host_layouts(inp):
    d = {}
    Wxp = np.asarray(inp["L0_attn_W_xproj"], np.float32)
    Wxp_r = np.concatenate([Wxp[64:80], Wxp[80:96], Wxp[0:64]], axis=0)
    d["wxp"] = np.ascontiguousarray(
        Wxp_r.reshape(96, 16, 128).transpose(2, 1, 0).reshape(128, 16 * 96))
    d["wdt"] = np.ascontiguousarray(np.asarray(inp["L0_attn_W_dt"], np.float32).T)
    d["bdt"] = np.ascontiguousarray(np.asarray(inp["L0_attn_b_dt"], np.float32).reshape(16, 128).T)
    d["dvec"] = np.ascontiguousarray(np.asarray(inp["L0_attn_D"], np.float32).reshape(16, 128).T)
    Alog = np.asarray(inp["L0_attn_A_log"], np.float32)
    pidx = np.arange(128)
    A_dev = np.empty((128, 256), np.float32)
    for dc in range(16):
        for j in range(16):
            A_dev[:, dc * 16 + j] = Alog[dc * 128 + 8 * j + (pidx >> 4), pidx & 15]
    d["adev"] = A_dev
    d["sug"] = np.ascontiguousarray(np.asarray(inp["L0_ffn_SU_ug"], np.float32).reshape(16, 128).T)
    ws = np.asarray(inp["L0_ffn_Wscale_ug"], np.float32)
    d["wsug"] = np.ascontiguousarray(
        np.concatenate([ws[:8192].reshape(64, 128).T, ws[8192:].reshape(64, 128).T], axis=1))
    d["sudp"] = np.ascontiguousarray(np.asarray(inp["L0_ffn_SU_dp"], np.float32).reshape(64, 128).T)
    d["wsdp"] = np.ascontiguousarray(np.asarray(inp["L0_ffn_Wscale_dp"], np.float32).reshape(16, 128).T)
    q = np.arange(128)
    r1 = np.arange(16)
    f1 = ((q[:, None] & 15) << 7) | ((q[:, None] >> 4) << 4) | r1[None, :]
    for name, key in (("L0_ffn_W_up", "wu"), ("L0_ffn_W_gate", "wg")):
        W = np.asarray(inp[name], np.float32).reshape(64, 128, 2048)
        d[key] = np.ascontiguousarray(W[:, :, f1].transpose(0, 2, 3, 1).reshape(64, 128, 2048))
    q2 = np.arange(128)
    r2 = np.arange(64)
    i2 = ((q2[:, None] & 63) << 7) | ((q2[:, None] >> 6) << 6) | r2[None, :]
    Wd = np.asarray(inp["L0_ffn_W_down"], np.float32).reshape(16, 128, 8192)
    d["wd"] = np.ascontiguousarray(Wd[:, :, i2].transpose(0, 2, 3, 1).reshape(16, 128, 8192))
    return d


def build_program():
    nc = bass.Bass("TRN2", target_bir_lowering=False)
    P = Prog()

    dram_in = lambda n, s, dt=F32: nc.dram_tensor(n, s, dt, kind="ExternalInput").ap()
    xT = dram_in("xT", [D, L])
    wxp = dram_in("wxp", [128, 16 * 96])
    wdt_d = dram_in("wdt", [64, 2048])
    bdt_d = dram_in("bdt", [128, 16])
    dvec_d = dram_in("dvec", [128, 16])
    adev_d = dram_in("adev", [128, 256])
    sug_d = dram_in("sug", [128, 16])
    wsug_d = dram_in("wsug", [128, 128])
    sudp_d = dram_in("sudp", [128, 64])
    wsdp_d = dram_in("wsdp", [128, 16])
    wu_d = dram_in("wu", [64, 128, 2048])
    wg_d = dram_in("wg", [64, 128, 2048])
    wd_d = dram_in("wd", [16, 128, 8192])
    h128f_d = dram_in("h128f", [128, 128])
    h128b_d = dram_in("h128b", [128, 128], BF16)
    s1b_d = dram_in("s1b", [128, 128], BF16)
    s2b_d = dram_in("s2b", [128, 128], BF16)
    rpb_d = dram_in("rpb", [128, 2048], BF16)
    sjb_d = dram_in("sjb", [128, 2048], BF16)
    repbb_d = dram_in("repbb", [32, 128], BF16)
    repcb_d = dram_in("repcb", [32, 128], BF16)
    ones128b_d = dram_in("ones128b", [128, 1], BF16)
    onesk1f_d = dram_in("onesk1f", [1, 128])

    _ik = "ExternalOutput" if DEBUG_TAPS else "Internal"
    if DEBUG_TAPS:
        dbg_brep = nc.dram_tensor("dbg_brep", [128, 2048], BF16, kind="ExternalOutput").ap()
        dbg_crep = nc.dram_tensor("dbg_crep", [128, 2048], BF16, kind="ExternalOutput").ap()
        dbg_dpre = nc.dram_tensor("dbg_dpre", [64, 2048], BF16, kind="ExternalOutput").ap()
        dbg_nlsp = nc.dram_tensor("dbg_nlsp", [128, 2048], BF16, kind="ExternalOutput").ap()
        dbg_ndx = nc.dram_tensor("dbg_ndx", [128, 2048], BF16, kind="ExternalOutput").ap()
        dbg_dA = nc.dram_tensor("dbg_dA", [128, 1024], F32, kind="ExternalOutput").ap()
        dbg_BX = nc.dram_tensor("dbg_BX", [128, 1024], BF16, kind="ExternalOutput").ap()
        dbg_h = nc.dram_tensor("dbg_h", [128, 1024], BF16, kind="ExternalOutput").ap()
        dbg_vhat = nc.dram_tensor("dbg_vhat", [128, 64 * 1024], BF16, kind="ExternalOutput").ap()
    xn_hbm = nc.dram_tensor("xn_hbm", [D, L], F32, kind=_ik).ap()
    s_hbm = nc.dram_tensor("s_hbm", [D, L], F32, kind=_ik).ap()
    xb2 = nc.dram_tensor("xb2", [128, 16 * 2048], BF16, kind=_ik).ap()
    vb2 = nc.dram_tensor("vb2", [128, 64 * 2048], BF16, kind=_ik).ap()
    out_d = nc.dram_tensor("out", [D, L], F32, kind="ExternalOutput").ap()

    import contextlib
    es = contextlib.ExitStack()
    sb = lambda n, s, dt: es.enter_context(nc.sbuf_tensor(n, s, dt))
    mksem = lambda n: Ctr(es.enter_context(nc.semaphore(n)))

    s2b = es.enter_context(nc.sbuf_tensor("s2b_s", [128, 128], BF16))
    wsdp_eff = es.enter_context(nc.sbuf_tensor("wsdp_eff", [128, 16], F32))
    import contextlib as _ctl
    p12 = _ctl.ExitStack()
    sb = lambda n, s, dt: p12.enter_context(nc.sbuf_tensor(n, s, dt))
    wxpb = sb("wxpb", [128, 1536], BF16)
    wdtb = sb("wdtb", [64, 2048], BF16)
    bdtn = sb("bdtn", [128, 16], F32)
    dcolv = sb("dcolv", [128, 16], F32)
    posA = sb("posA", [128, 256], F32)
    h128su = sb("h128su", [128, 2048], BF16)
    h128b = sb("h128b_s", [128, 128], BF16)
    s1b = sb("s1b_s", [128, 128], BF16)
    rpb = sb("rpb_s", [128, 2048], BF16)
    sjb = sb("sjb_s", [128, 2048], BF16)
    repbb = sb("repbb_s", [32, 128], BF16)
    repcb = sb("repcb_s", [32, 128], BF16)
    ones128b = sb("ones128b_s", [128, 1], BF16)
    onesk1f = sb("onesk1f_s", [1, 128], F32)
    wsu_eff = sb("wsu_eff", [128, 64], F32)
    wsg_eff = sb("wsg_eff", [128, 64], F32)
    su1c = sb("su1c", [128, 16], F32)
    w1col = sb("w1col", [96, 1], F32)
    BC32 = sb("BC32", [32, 2048], BF16)
    dpreb = sb("dpreb", [64, 2048], BF16)
    B_rep = sb("B_rep", [128, 2048], BF16)
    C_rep = sb("C_rep", [128, 2048], BF16)

    c_ld = mksem("c_ld")
    c_x = mksem("c_x")
    c_t = mksem("c_t")
    c_s = mksem("c_s")
    c_v = mksem("c_v")
    c_g = mksem("c_g")
    c_xnst = mksem("c_xnst")
    c_xbst = mksem("c_xbst")
    c_xbld = mksem("c_xbld")
    c_w = mksem("c_w")
    c_xnld = mksem("c_xnld")
    c_vbst = mksem("c_vbst")
    c_sst = mksem("c_sst")
    c_vbld = mksem("c_vbld")
    c_wd = mksem("c_wd")
    c_xs = mksem("c_xs")
    c_ost = mksem("c_ost")
    c_ph = mksem("c_ph")
    c_dbg = mksem("c_dbg")

    M = P.marks

    def p0():
        loads = [
            (bdtn, bdt_d), (dcolv, dvec_d), (posA, adev_d), (su1c, sug_d),
            (h128b, h128b_d), (s1b, s1b_d), (s2b, s2b_d), (rpb, rpb_d),
            (sjb, sjb_d), (repbb, repbb_d), (repcb, repcb_d),
            (ones128b, ones128b_d), (onesk1f, onesk1f_d), (sudp_s, sudp_d),
            (h128f_s, h128f_d), (wdtb if False else None, None),
        ]
        for dst, src in loads:
            if dst is None:
                continue
            P.dma("sync", lambda e, d=dst, s=src: e.dma_start(out=d[:], in_=s), c_ld)
        P.dma("sync", lambda e: e.dma_start(out=wsu_eff[:], in_=bass.AP(wsug_d.tensor, 0, [[128, 128], [1, 64]])), c_ld)
        P.dma("sync", lambda e: e.dma_start(out=wsg_eff[:], in_=bass.AP(wsug_d.tensor, 64, [[128, 128], [1, 64]])), c_ld)
        P.dma("sync", lambda e: e.dma_start(out=wsdp_eff[:], in_=wsdp_d), c_ld)
        P.dma("sync", lambda e: e.dma_start(out=murep[:, 0:1536], in_=wxp), c_ld)
        P.dma("sync", lambda e: e.dma_start(out=xdblt[0:64, :], in_=wdt_d), c_ld)
        P.mk("p0_ld", c_ld)

        P.w("scalar", c_ld, "p0_ld")
        P.do("scalar", lambda e: e.activation(out=posA[:], in_=posA[:], func=AF.Exp), c_s)
        P.do("scalar", lambda e: e.activation(out=posA[:], in_=posA[:], func=AF.Copy, scale=-1.0), c_s)
        P.do("scalar", lambda e: e.activation(out=su1c[:], in_=su1c[:], func=AF.Copy,
                                              scale=float(1.0 / (64.0 * np.sqrt(2048.0)))), c_s)
        P.do("scalar", lambda e: e.activation(out=wsg_eff[:], in_=wsg_eff[:], func=AF.Copy, scale=64.0), c_s)
        P.do("scalar", lambda e: e.activation(out=wsdp_eff[:], in_=wsdp_eff[:], func=AF.Copy, scale=64.0), c_s)
        P.do("scalar", lambda e: e.copy(out=wxpb[:], in_=murep[:, 0:1536]), c_s)
        P.do("scalar", lambda e: e.copy(out=wdtb[:], in_=xdblt[0:64, :]), c_s, mark="p0_s")

        P.w("vector", c_s, "p0_s")
        P.do("vector", lambda e: e.tensor_tensor(out=wsu_eff[:], in0=wsu_eff[:], in1=sudp_s[:], op=OP.mult), c_v)
        P.do("vector", lambda e: e.tensor_scalar_mul(out=wsu_eff[:], in0=wsu_eff[:],
                                                     scalar1=float(1.0 / np.sqrt(8192.0))), c_v)
        for fc in range(16):
            P.do("vector", lambda e, fc=fc: e.tensor_scalar_mul(
                out=h128su[:, fc * 128:(fc + 1) * 128], in0=h128f_s[:],
                scalar1=su1c[:, fc:fc + 1]), c_v, mark="p0_v")

    xhsc = contextlib.ExitStack()
    xhat = xhsc.enter_context(nc.sbuf_tensor("xhat", [128, 16 * 2048], BF16))
    p1sc = contextlib.ExitStack()
    sb1 = lambda n, s, dt: p1sc.enter_context(nc.sbuf_tensor(n, s, dt))
    sudp_s = p1sc.enter_context(nc.sbuf_tensor("sudp_s", [128, 64], F32))
    h128f_s = p1sc.enter_context(nc.sbuf_tensor("h128f_s", [128, 128], F32))
    murep = sb1("murep", [128, 2048], F32)
    rrep = sb1("rrep", [128, 2048], F32)
    xdblt = sb1("xdblt", [96, 2048], F32)
    tmp96 = sb1("tmp96", [96, 2048], F32)
    xt = [sb1(f"xt{i}", [128, 2048], F32) for i in range(2)]
    xbt = [sb1(f"xbt{i}", [128, 2048], BF16) for i in range(2)]
    xsq = [sb1(f"xsq{i}", [128, 2048], BF16) for i in range(2)]
    xnf = [sb1(f"xnf{i}", [128, 2048], F32) for i in range(2)]
    xpa = sb1("xpa", [128, 2048], BF16)
    xbst = [sb1(f"xbst{i}", [128, 4096], BF16) for i in range(2)]

    ps1 = contextlib.ExitStack()
    ps_sum = ps1.enter_context(nc.psum_tensor("ps_sum", [1, 2048], F32))
    ps_sq = ps1.enter_context(nc.psum_tensor("ps_sq", [1, 2048], F32))

    def p1a():
        for dc in range(NDC):
            b = dc % 2
            if dc >= 2:
                P.w("sync", c_s, f"p1a_s_{dc-2}")
                P.w("sync", c_v, f"p1a_v_{dc-2}")
            P.dma("sync", lambda e, b=b, dc=dc: e.dma_start(out=xt[b][:], in_=xT[dc * 128:(dc + 1) * 128, :]),
                  c_x, mark=f"p1a_x_{dc}")
            P.w("scalar", c_x, f"p1a_x_{dc}")
            if dc >= 2:
                P.w("scalar", c_t, f"p1a_t_{dc-2}")
            P.do("scalar", lambda e, b=b: e.copy(out=xbt[b][:], in_=xt[b][:]), c_s, mark=f"p1a_s_{dc}")
            P.w("vector", c_x, f"p1a_x_{dc}")
            if dc >= 2:
                P.w("vector", c_t, f"p1a_t_{dc-2}")
            P.do("vector", lambda e, b=b: e.tensor_tensor(out=xsq[b][:], in0=xt[b][:], in1=xt[b][:], op=OP.mult),
                 c_v, mark=f"p1a_v_{dc}")
            P.w("tensor", c_s, f"p1a_s_{dc}")
            P.w("tensor", c_v, f"p1a_v_{dc}")
            for c in range(4):
                sl = slice(c * 512, (c + 1) * 512)
                P.do("tensor", lambda e, b=b, dc=dc, sl=sl: e.matmul(
                    ps_sum[0:1, sl], ones128b[:], xbt[b][:, sl],
                    start=(dc == 0), stop=(dc == NDC - 1)), c_t)
                P.do("tensor", lambda e, b=b, dc=dc, sl=sl: e.matmul(
                    ps_sq[0:1, sl], ones128b[:], xsq[b][:, sl],
                    start=(dc == 0), stop=(dc == NDC - 1)), c_t,
                    mark=f"p1a_t_{dc}" if c == 3 else None)

    def p1a_epi(prep1, pw1):
        P.w("vector", c_t, "p1a_t_15")
        P.do("vector", lambda e: e.tensor_scalar_mul(out=tmp96[0:1, :], in0=ps_sum[0:1, :], scalar1=float(1.0 / D)), c_v)
        P.do("vector", lambda e: e.tensor_scalar_mul(out=xdblt[0:1, :], in0=ps_sq[0:1, :], scalar1=float(1.0 / D)),
             c_v, mark="rows_v")
        P.w("scalar", c_v, "rows_v")
        P.do("scalar", lambda e: e.activation(out=murep[0:1, :], in_=tmp96[0:1, :], func=AF.Square), c_s, mark="musq_s")
        P.w("vector", c_s, "musq_s")
        P.do("vector", lambda e: e.tensor_tensor(out=xdblt[0:1, :], in0=xdblt[0:1, :], in1=murep[0:1, :], op=OP.subtract), c_v)
        P.do("vector", lambda e: e.tensor_scalar_add(out=xdblt[0:1, :], in0=xdblt[0:1, :], scalar1=1e-5),
             c_v, mark="var_v")
        P.w("scalar", c_v, "var_v")
        P.do("scalar", lambda e: e.activation(out=xdblt[0:1, :], in_=xdblt[0:1, :], func=AF.Sqrt),
             c_s, mark="sqrt_s")
        P.w("vector", c_s, "sqrt_s")
        P.do("vector", lambda e: e.reciprocal(out=xdblt[0:1, :], in_=xdblt[0:1, :]), c_v, mark="rrow_v")
        P.w("tensor", c_v, "rrow_v")
        k = 0
        for dst, rowsrc in ((murep, tmp96), (rrep, xdblt)):
            for c in range(4):
                sl = slice(c * 512, (c + 1) * 512)
                if k >= 1:
                    P.w("tensor", c_s, f"repev_{k-1}")
                P.do("tensor", lambda e, rowsrc=rowsrc, sl=sl: e.matmul(
                    prep1[:], onesk1f[:], rowsrc[0:1, sl], start=True, stop=True),
                    c_t, mark=f"repmm_{k}")
                P.w("scalar", c_t, f"repmm_{k}")
                P.do("scalar", lambda e, dst=dst, sl=sl: e.copy(out=dst[:, sl], in_=prep1[:]),
                     c_s, mark=f"repev_{k}")
                k += 1


    def p1b(ps_xd, ppa):
        for dc in range(NDC):
            b = dc % 2
            if dc < 2:
                P.w("sync", c_s, "p1a_s_15")
                P.w("sync", c_v, "p1a_v_15")
            if dc >= 2:
                P.w("sync", c_v, f"p1b_xn_{dc-2}")
            P.dma("sync", lambda e, b=b, dc=dc: e.dma_start(out=xt[b][:], in_=xT[dc * 128:(dc + 1) * 128, :]),
                  c_x, mark=f"p1b_x_{dc}")
            P.w("vector", c_x, f"p1b_x_{dc}")
            if dc == 0:
                P.w("vector", c_s, "repev_7")
            if dc >= 2:
                P.w("vector", c_xnst, f"p1b_xnst_{dc-2}")
                P.w("vector", c_s, f"p1b_cast_{dc-2}")
            P.do("vector", lambda e, b=b: e.tensor_tensor(out=xnf[b][:], in0=xt[b][:], in1=murep[:], op=OP.subtract), c_v)
            P.do("vector", lambda e, b=b: e.tensor_tensor(out=xnf[b][:], in0=xnf[b][:], in1=rrep[:], op=OP.mult),
                 c_v, mark=f"p1b_xn_{dc}")
            P.w("gpsimd", c_v, f"p1b_xn_{dc}")
            P.dma("gpsimd", lambda e, b=b, dc=dc: e.dma_start(out=xn_hbm[dc * 128:(dc + 1) * 128, :], in_=xnf[b][:]),
                  c_xnst, mark=f"p1b_xnst_{dc}")
            P.w("scalar", c_v, f"p1b_xn_{dc}")
            if dc >= 2:
                P.w("scalar", c_t, f"p1b_t_{dc-2}")
            P.do("scalar", lambda e, b=b: e.copy(out=xbt[b][:], in_=xnf[b][:]), c_s, mark=f"p1b_cast_{dc}")
            P.w("tensor", c_s, f"p1b_cast_{dc}")
            for c in range(4):
                sl = slice(c * 512, (c + 1) * 512)
                P.do("tensor", lambda e, b=b, dc=dc, sl=sl: e.matmul(
                    ps_xd[0:96, sl], wxpb[:, dc * 96:(dc + 1) * 96], xbt[b][:, sl],
                    start=(dc == 0), stop=(dc == NDC - 1)), c_t)
            for c in range(4):
                sl = slice(c * 512, (c + 1) * 512)
                k = dc * 4 + c
                if k >= 2:
                    P.w("tensor", c_s, f"p1b_pa_{k-2}")
                P.do("tensor", lambda e, b=b, dc=dc, sl=sl, c=c: e.matmul(
                    ppa[c % 2][:], h128su[:, dc * 128:(dc + 1) * 128], xbt[b][:, sl],
                    start=True, stop=True), c_t, mark=f"p1b_pamm_{k}" + ("" if c < 3 else "_last"))
                if c == 3:
                    P.mk(f"p1b_t_{dc}", c_t)
                P.w("scalar", c_t, f"p1b_pamm_{k}" + ("" if c < 3 else "_last"))
                if dc >= 1 and c == 0:
                    P.w("scalar", c_xbst, f"p1b_xbst_{dc-1}")
                P.do("scalar", lambda e, sl=sl, c=c: e.copy(out=xpa[:, sl], in_=ppa[c % 2][:]),
                     c_s, mark=f"p1b_pa_{k}")
            P.w("gpsimd", c_s, f"p1b_pa_{dc*4+3}")
            for g in range(8):
                P.dma("gpsimd", lambda e, g=g, dc=dc: e.dma_start(
                    out=bass.AP(xb2.tensor, (g * 16 + dc) * 16 * 2048, [[2048, 16], [1, 2048]]),
                    in_=xpa[g * 16:(g + 1) * 16, :]),
                    c_xbst, mark=f"p1b_xbst_{dc}" if g == 7 else None)

    def p1b_epi(ps_xd, ppa):
        P.w("vector", c_t, "p1b_t_15")
        P.do("vector", lambda e: e.tensor_copy(xdblt[:], ps_xd[0:96, :]), c_v, mark="xdbl_v")
        P.w("scalar", c_v, "xdbl_v")
        P.do("scalar", lambda e: e.copy(out=BC32[:], in_=xdblt[0:32, :]), c_s)
        P.do("scalar", lambda e: e.copy(out=dpreb[0:32, :], in_=xdblt[32:64, :]), c_s)
        P.do("scalar", lambda e: e.copy(out=dpreb[32:64, :], in_=xdblt[64:96, :]), c_s, mark="bc_s")
        P.w("tensor", c_s, "bc_s")
        k = 0
        for st, dst in ((repbb, B_rep), (repcb, C_rep)):
            for c in range(4):
                sl = slice(c * 512, (c + 1) * 512)
                if k >= 2:
                    P.w("tensor", c_s, f"bcrep_s_{k-2}")
                P.do("tensor", lambda e, st=st, sl=sl, k=k: e.matmul(
                    ppa[k % 2][:], st[:], BC32[:, sl], start=True, stop=True), c_t, mark=f"bcrep_t_{k}")
                P.w("scalar", c_t, f"bcrep_t_{k}")
                P.do("scalar", lambda e, dst=dst, sl=sl, k=k: e.copy(out=dst[:, sl], in_=ppa[k % 2][:]),
                     c_s, mark=f"bcrep_s_{k}")
                k += 1

    def p1c(pxh):
        if DEBUG_TAPS:
            P.w("sync", c_s, "bcrep_s_7")
            P.dma("sync", lambda e: e.dma_start(out=dbg_brep[:], in_=B_rep[:]), c_dbg)
            P.dma("sync", lambda e: e.dma_start(out=dbg_crep[:], in_=C_rep[:]), c_dbg)
            P.dma("sync", lambda e: e.dma_start(out=dbg_dpre[:], in_=dpreb[:]), c_dbg)
        P.w("sync", c_xbst, f"p1b_xbst_{NDC-1}")
        P.w("tensor", c_s, "bcrep_s_7")
        P.w("tensor", c_v, "xdbl_v")
        for k in range(8):
            if k >= 2:
                P.w("sync", c_t, f"p1c_t_{k-2}")
            P.dma("sync", lambda e, k=k: e.dma_start(
                out=xbst[k % 2][:], in_=bass.AP(xb2.tensor, k * 4096, [[32768, 128], [1, 4096]])),
                c_xbld, mark=f"p1c_ld_{k}")
            P.w("tensor", c_xbld, f"p1c_ld_{k}")
            for c in range(8):
                nmm = 8 * k + c
                if nmm >= 2:
                    P.w("tensor", c_s, f"p1c_s_{nmm-2}")
                P.do("tensor", lambda e, k=k, c=c, nmm=nmm: e.matmul(
                    pxh[nmm % 2][:], s1b[:], xbst[k % 2][:, c * 512:(c + 1) * 512],
                    start=True, stop=True), c_t, mark=f"p1c_mm_{nmm}")
                if c == 7:
                    P.mk(f"p1c_t_{k}", c_t)
                P.w("scalar", c_t, f"p1c_mm_{nmm}")
                P.do("scalar", lambda e, nmm=nmm: e.copy(
                    out=xhat[:, nmm * 512:(nmm + 1) * 512], in_=pxh[nmm % 2][:]),
                    c_s, mark=f"p1c_s_{nmm}")
        P.mk("p1c_s_done", c_s)
        P.mk("p1c_t_done", c_t)
        P.mk("p1c_v_done", c_v)

    def build_p2(bufs):
        (wub, wgb, ub, sgb, vtile, vA, xnf2, nlsp, ndx, dAh, BXh, hCs,
         stcol, sigt, sst, pu, pg, ppa2, prep, prep3, py) = bufs
        prepd = [prep[0], ppa2]
        ppa3 = prep3
        prepx = [prep[1], py]

        def tok_Fw(hi):
            if hi >= 2:
                P.w("gpsimd", c_t, f"fdone_{hi-2}")
            P.dma("gpsimd", lambda e, hi=hi: e.dma_start(
                out=wub[hi % 2][:], in_=bass.AP(wu_d.tensor, hi * 262144, [[2048, 128], [1, 2048]])), c_w)
            P.dma("gpsimd", lambda e, hi=hi: e.dma_start(
                out=wgb[hi % 2][:], in_=bass.AP(wg_d.tensor, hi * 262144, [[2048, 128], [1, 2048]])),
                c_w, mark=f"w_ld_{hi}")

        def tok_Fa(hi, c, hf):
            wpar = hi % 2
            k = hi * 4 + c
            if hf == 0:
                if c == 0:
                    P.w("tensor", c_w, f"w_ld_{hi}")
                    if hi == 0:
                        P.w("tensor", c_s, "p1c_s_done")
                if k >= 2:
                    P.w("tensor", c_s, f"uev_{k-2}")
            def mm_up(e, wpar=wpar, c=c, hf=hf):
                for r1 in range(4 * hf, 4 * hf + 4):
                    inst = e.matmul(pu[c % 2][:], wub[wpar][:, r1 * 128:(r1 + 1) * 128],
                                    xhat[:, r1 * 2048 + c * 512: r1 * 2048 + (c + 1) * 512],
                                    start=(r1 == 0), stop=(r1 == 15))
                return inst
            if hf == 3:
                P.do("tensor", mm_up, c_t, mark=f"upmm_{k}")
            else:
                P.do("tensor", mm_up)

        def tok_Fb(hi, c, hf):
            wpar = hi % 2
            k = hi * 4 + c
            if hf == 0 and k >= 1:
                P.w("tensor", c_s, f"gev_{k-1}")
            def mm_g(e, wpar=wpar, c=c, hf=hf):
                for r1 in range(4 * hf, 4 * hf + 4):
                    inst = e.matmul(pg[c % 2][:], wgb[wpar][:, r1 * 128:(r1 + 1) * 128],
                                    xhat[:, r1 * 2048 + c * 512: r1 * 2048 + (c + 1) * 512],
                                    start=(r1 == 0), stop=(r1 == 15))
                return inst
            if hf == 3:
                P.do("tensor", mm_g, c_t, mark=f"gmm_{k}")
            else:
                P.do("tensor", mm_g)

        def tok_F(hi, c):
            k = hi * 4 + c
            sl = slice(c * 512, (c + 1) * 512)
            P.w("scalar", c_t, f"upmm_{k}")
            if k >= 4:
                P.w("scalar", c_v, f"vtt_{k-4}")
            P.do("scalar", lambda e, hi=hi, sl=sl, c=c: e.activation(
                out=ub[:, sl], in_=pu[c % 2][:], func=AF.Copy, scale=wsu_eff[:, hi:hi + 1]),
                c_s, mark=f"uev_{k}")
            P.w("scalar", c_t, f"gmm_{k}")
            P.do("scalar", lambda e, hi=hi, sl=sl, c=c: e.activation(
                out=sgb[:, sl], in_=pg[c % 2][:], func=AF.Copy, scale=wsg_eff[:, hi:hi + 1]),
                c_s, mark=f"gev_{k}")
            if c == 3:
                P.do("scalar", lambda e, hi=hi: e.activation(
                    out=sgb[:], in_=sgb[:], func=AF.Silu), c_s, mark=f"slu_{hi}")

        def tok_Fpa(hi, hp):
            for c in (2 * hp, 2 * hp + 1):
                k = hi * 4 + c
                sl = slice(c * 512, (c + 1) * 512)
                P.w("vector", c_s, f"slu_{hi}")
                if hi >= 1 and c == 0:
                    P.w("vector", c_t, f"pamm_{(hi-1)*4+3}")
                    P.w("vector", c_vbst, f"vbst_{hi-1}")
                P.do("vector", lambda e, sl=sl: e.tensor_tensor(out=vtile[:, sl], in0=sgb[:, sl], in1=ub[:, sl], op=OP.mult),
                     c_v, mark=f"vtt_{k}")
                P.w("tensor", c_v, f"vtt_{k}")
                if k >= 1:
                    P.w("tensor", c_s, f"paev_{k-1}")
                P.do("tensor", lambda e, sl=sl: e.matmul(ppa3[:], h128b[:], vtile[:, sl], start=True, stop=True),
                     c_t, mark=f"pamm_{k}")
                if c == 3:
                    P.mk(f"fdone_{hi}", c_t)
                P.w("scalar", c_t, f"pamm_{k}")
                if hi >= 1 and c == 0:
                    P.w("scalar", c_vbst, f"vbst_{hi-1}")
                P.do("scalar", lambda e, sl=sl: e.copy(out=vA[:, sl], in_=ppa3[:]), c_s, mark=f"paev_{k}")

        def tok_Fsync(hi):
            P.w("sync", c_s, f"paev_{hi*4+3}")
            for b6 in range(2):
                P.dma("sync", lambda e, b6=b6, hi=hi: e.dma_start(
                    out=bass.AP(vb2.tensor, (b6 * 64 + hi) * 131072, [[2048, 64], [1, 2048]]),
                    in_=vA[b6 * 64:(b6 + 1) * 64, :]),
                    c_vbst, mark=f"vbst_{hi}" if b6 == 1 else None)

        def tok_Mdp(dc, c):
            if dc == 0 and c == 0:
                P.w("tensor", c_s, "p1c_s_done")
                P.w("tensor", c_v, "xdbl_v")
            if c == 0:
                if dc >= 2:
                    P.w("gpsimd", c_v, f"sev_{(dc-2)*4+3}")
                P.w("gpsimd", c_xnst, f"p1b_xnst_{dc}")
                P.dma("gpsimd", lambda e, dc=dc: e.dma_start(
                    out=xnf2[dc % 2][:], in_=xn_hbm[dc * 128:(dc + 1) * 128, :]), c_xnld, mark=f"xnld_{dc}")
            if dc >= 1:
                P.w("tensor", c_s, f"dA_{64*dc - 2 + (c % 2)}")
            if dc * 4 + c >= 2:
                P.w("tensor", c_s, f"softp_{dc*4+c-2}")
            P.do("tensor", lambda e, dc=dc, c=c: e.matmul(
                prepd[c % 2][:], wdtb[:, dc * 128:(dc + 1) * 128],
                dpreb[:, c * 512:(c + 1) * 512], start=True, stop=True), c_t, mark=f"dproj_{dc*4+c}")
            P.w("scalar", c_t, f"dproj_{dc*4+c}")
            if dc >= 2 and c == 0:
                P.w("scalar", c_t, f"repd_{64*(dc-1)-1}")
                P.w("scalar", c_v, f"ndx_{dc-2}")
            P.do("scalar", lambda e, c=c, dc=dc: e.activation(
                out=sigt[:], in_=prepd[c % 2][:], func=AF.Exp,
                bias=bdtn[:, dc:dc + 1]), c_s)
            P.do("scalar", lambda e, c=c, dc=dc: e.activation(
                out=nlsp[dc % 2][:, c * 512:(c + 1) * 512], in_=sigt[:], func=AF.Ln, bias=1.0),
                c_s, mark=f"softp_{dc*4+c}")

        def tok_Msp(dc):
            P.w("vector", c_s, f"softp_{dc*4+3}")
            P.w("vector", c_xnld, f"xnld_{dc}")
            if dc >= 1:
                P.w("vector", c_t, f"repx_{64*dc-1}")
            P.do("vector", lambda e, dc=dc: e.tensor_tensor(
                out=ndx[dc % 2][:], in0=nlsp[dc % 2][:], in1=xnf2[dc % 2][:], op=OP.mult),
                c_v, mark=f"ndx_{dc}")

        def tok_Mtile(dc, half, j):
            ti = (dc * 2 + half) * 16 + j
            par = ti % 3
            for cc in range(2):
                g = ti * 2 + cc
                if g % 64 < 2:
                    P.w("tensor", c_s, f"softp_{dc*4+3}")
                if g >= 2:
                    P.w("tensor", c_s, f"dA_{g-2}")
                P.do("tensor", lambda e, j=j, half=half, cc=cc, g=g, dc=dc: e.matmul(
                    prepd[g % 2][:], rpb[:, j * 128:(j + 1) * 128],
                    nlsp[dc % 2][:, half * 1024 + cc * 512: half * 1024 + (cc + 1) * 512],
                    start=True, stop=True), c_t, mark=f"repd_{g}")
                if g % 64 == 0:
                    P.w("tensor", c_v, f"ndx_{dc}")
                if g % 32 == 1:
                    key = f"sev_{dc*4 + half*2 - 1}" if (dc, half) != (0, 0) else None
                    if key:
                        P.w("tensor", c_v, key)
                if g >= 2:
                    P.w("tensor", c_v, f"BX_{g-2}")
                P.do("tensor", lambda e, j=j, half=half, cc=cc, g=g, dc=dc: e.matmul(
                    prepx[g % 2][:], rpb[:, j * 128:(j + 1) * 128],
                    ndx[dc % 2][:, half * 1024 + cc * 512: half * 1024 + (cc + 1) * 512],
                    start=True, stop=True), c_t, mark=f"repx_{g}")
                P.w("scalar", c_t, f"repd_{g}")
                if ti >= 3 and cc == 0:
                    P.w("scalar", c_v, f"scan_{ti-3}")
                P.do("scalar", lambda e, par=par, cc=cc, dc=dc, j=j, g=g: e.activation(
                    out=dAh[par][:, cc * 512:(cc + 1) * 512], in_=prepd[g % 2][:], func=AF.Exp,
                    scale=posA[:, dc * 16 + j: dc * 16 + j + 1]), c_s, mark=f"dA_{g}")
                P.w("vector", c_t, f"repx_{g}")
                P.do("vector", lambda e, par=par, half=half, cc=cc, g=g: e.tensor_tensor(
                    out=BXh[par][:, cc * 512:(cc + 1) * 512], in0=prepx[g % 2][:],
                    in1=B_rep[:, half * 1024 + cc * 512: half * 1024 + (cc + 1) * 512], op=OP.mult),
                    c_v, mark=f"BX_{g}")
            if DEBUG_TAPS and ti == 0:
                P.w("sync", c_s, "dA_1")
                P.dma("sync", lambda e: e.dma_start(out=dbg_dA[:], in_=dAh[0][:]), c_dbg)
                P.w("sync", c_v, "BX_1")
                P.dma("sync", lambda e: e.dma_start(out=dbg_BX[:], in_=BXh[0][:]), c_dbg, mark="dbg_t0")
            P.w("vector", c_s, f"dA_{ti*2+1}")
            hi_half = dc * 2 + half
            if hi_half >= 1:
                P.w("vector", c_t, f"yfree_{hi_half-1}", opt=True)
            def scan_fn(e, par=par, half=half, j=j):
                init = 0.0 if half == 0 else stcol[:, j:j + 1]
                return e.tensor_tensor_scan(out=hCs[j][:], data0=dAh[par][:], data1=BXh[par][:],
                                            initial=init, op0=OP.mult, op1=OP.add)
            P.do("vector", scan_fn, c_v, mark=f"scan_{ti}")
            if DEBUG_TAPS and ti == 0:
                P.w("sync", c_v, "scan_0")
                P.dma("sync", lambda e: e.dma_start(out=dbg_h[:], in_=hCs[0][:]), c_dbg, mark="dbg_h0")
                P.w("vector", c_dbg, "dbg_h0")
            if DEBUG_TAPS and ti == 2:
                P.w("vector", c_dbg, "dbg_t0")
                P.w("scalar", c_dbg, "dbg_t0")
            if half == 0:
                P.do("vector", lambda e, j=j: e.tensor_copy(stcol[:, j:j + 1], hCs[j][:, 1023:1024]), c_v)
            P.do("vector", lambda e, half=half, j=j: e.tensor_tensor(
                out=hCs[j][:], in0=hCs[j][:], in1=C_rep[:, half * 1024:(half + 1) * 1024], op=OP.mult),
                c_v, mark=f"hC_{ti}")

        def tok_My(dc, half):
            hi_half = dc * 2 + half
            for q in range(2):
                qq = half * 2 + q
                kq = dc * 4 + qq
                if kq >= 1:
                    P.w("tensor", c_v, f"sev_{kq-1}", opt=True)
                if q == 0:
                    P.w("tensor", c_v, f"hC_{hi_half*16+15}")
                def mm_y(e, hi_half=hi_half, q=q):
                    for j in range(16):
                        inst = e.matmul(py[:], sjb[:, j * 128:(j + 1) * 128],
                                        hCs[j][:, q * 512:(q + 1) * 512],
                                        start=(j == 0), stop=(j == 15))
                    return inst
                P.do("tensor", mm_y, c_t, mark=f"ymm_{kq}")
                if q == 1:
                    P.mk(f"yfree_{hi_half}", c_t)
                P.w("vector", c_t, f"ymm_{kq}")
                if kq >= 1:
                    P.w("vector", c_sst, f"sst_{kq-1}", opt=True)
                P.do("vector", lambda e, dc=dc, qq=qq: e.scalar_tensor_tensor(
                    out=sst[qq % 2][:], in0=xnf2[dc % 2][:, qq * 512:(qq + 1) * 512],
                    scalar=dcolv[:, dc:dc + 1], in1=py[:], op0=OP.mult, op1=OP.add),
                    c_v, mark=f"sev_{kq}")
                P.w("sync", c_v, f"sev_{kq}")
                P.dma("sync", lambda e, dc=dc, qq=qq: e.dma_start(
                    out=bass.AP(s_hbm.tensor, dc * 262144 + qq * 512, [[2048, 128], [1, 512]]),
                    in_=sst[qq % 2][:]), c_sst, mark=f"sst_{kq}")

        def interleave(a, b):
            out = []
            na, nb = len(a), len(b)
            wa = na * 1.25
            ia = ib = 0
            for k in range(na + nb):
                if ia < na and ia * nb <= ib * wa:
                    out.append(a[ia]); ia += 1
                elif ib < nb:
                    out.append(b[ib]); ib += 1
                else:
                    out.append(a[ia]); ia += 1
            return out

        for dc in range(NDC):
            mix = []
            if dc == 0:
                mix += [lambda c=c: tok_Mdp(0, c) for c in range(4)]
                mix.append(lambda: tok_Msp(0))
            for half in range(2):
                for j in range(16):
                    mix.append(lambda dc=dc, half=half, j=j: tok_Mtile(dc, half, j))
                if half == 1 and dc + 1 < NDC:
                    mix += [lambda dc=dc, c=c: tok_Mdp(dc + 1, c) for c in range(4)]
                    mix.append(lambda dc=dc: tok_Msp(dc + 1))
                mix.append(lambda dc=dc, half=half: tok_My(dc, half))
            ffn = []
            for k in range(4):
                hi = dc * 4 + k
                ffn.append(lambda hi=hi: tok_Fw(hi))
                for c in range(4):
                    for hf in range(4):
                        ffn.append(lambda hi=hi, c=c, hf=hf: tok_Fa(hi, c, hf))
                    for hf in range(4):
                        ffn.append(lambda hi=hi, c=c, hf=hf: tok_Fb(hi, c, hf))
                    ffn.append(lambda hi=hi, c=c: tok_F(hi, c))
                    if c == 3:
                        ffn.append(lambda hi=hi: tok_Fpa(hi, 0))
                        ffn.append(lambda hi=hi: tok_Fpa(hi, 1))
                ffn.append(lambda hi=hi: tok_Fsync(hi))
            for tok in interleave(mix, ffn):
                tok()
        P.mk("p2_t_done", c_t)
        P.mk("p2_s_done", c_s)
        P.mk("p2_v_done", c_v)

    def build_p3(bufs):
        (vhat, vbst_t, wdb, gt, xt3, st3, ot3, ppb, pdn) = bufs

        def emit_pb_chunk(tb, rr):
            kk = tb * 32 + rr
            if rr >= 1:
                P.w("sync", c_t, f"pbmm_{kk-1}")
            P.dma("sync", lambda e, rr=rr, tb=tb: e.dma_start(
                out=vbst_t[kk % 2][:], in_=bass.AP(vb2.tensor, 2 * rr * 2048 + tb * 1024,
                                                   [[131072, 128], [2048, 2], [1, 1024]])),
                c_vbld, mark=f"vbld_{kk}")
            P.w("tensor", c_vbld, f"vbld_{kk}")
            if kk >= 1:
                P.w("tensor", c_s, f"pbev_{kk-1}")
            def mm_pb(e, kk=kk):
                for sub in range(2):
                    for half in range(2):
                        inst = e.matmul(ppb[sub][:, half * 512:(half + 1) * 512], s2b[:],
                                        vbst_t[kk % 2][:, sub * 1024 + half * 512:
                                                        sub * 1024 + (half + 1) * 512],
                                        start=True, stop=True)
                return inst
            P.do("tensor", mm_pb, c_t, mark=f"pbmm_{kk}")
            P.w("scalar", c_t, f"pbmm_{kk}")
            P.do("scalar", lambda e, rr=rr: e.copy(
                out=vhat[:, (2 * rr) * 1024:(2 * rr + 1) * 1024], in_=ppb[0][:]), c_s)
            P.do("scalar", lambda e, rr=rr: e.copy(
                out=vhat[:, (2 * rr + 1) * 1024:(2 * rr + 2) * 1024], in_=ppb[1][:]), c_s,
                mark=f"pbev_{kk}")

        def emit_dn_grp(tb, hc, grp, chase):
            k = tb * 16 + hc
            kg = k * 4 + grp
            if kg >= 2:
                P.w("gpsimd", c_t, f"dng_{kg-2}")
            P.dma("gpsimd", lambda e, hc=hc, grp=grp: e.dma_start(
                out=wdb[grp % 2][:], in_=bass.AP(wd_d.tensor, hc * 1048576 + grp * 2048,
                                                 [[8192, 128], [1, 2048]])),
                c_wd, mark=f"wdld_{kg}")
            if chase:
                P.w("tensor", c_s, f"pbev_{tb*32 + 8*grp + 7}")
            if grp == 0 and k >= 1:
                P.w("tensor", c_s, f"sil_{k-1}")
            P.w("tensor", c_wd, f"wdld_{kg}")
            def mm_dn(e, hc=hc, grp=grp):
                for rr in range(16):
                    r2 = grp * 16 + rr
                    for half in range(2):
                        inst = e.matmul(pdn[hc % 2][:, half * 512:(half + 1) * 512],
                                        wdb[grp % 2][:, rr * 128:(rr + 1) * 128],
                                        vhat[:, r2 * 1024 + half * 512: r2 * 1024 + (half + 1) * 512],
                                        start=(r2 == 0), stop=(r2 == 63))
                return inst
            P.do("tensor", mm_dn, c_t, mark=f"dng_{kg}")
            if grp == 3:
                P.mk(f"dnmm_{k}", c_t)

        def emit_dn_epi(tb, hc):
            k = tb * 16 + hc
            if k >= 2:
                P.w("sync", c_v, f"comb_{k-2}")
            P.dma("sync", lambda e, hc=hc, tb=tb: e.dma_start(
                out=xt3[hc % 2][:], in_=bass.AP(xT.tensor, hc * 262144 + tb * 1024, [[2048, 128], [1, 1024]])), c_xs)
            P.dma("sync", lambda e, hc=hc, tb=tb: e.dma_start(
                out=st3[hc % 2][:], in_=bass.AP(s_hbm.tensor, hc * 262144 + tb * 1024, [[2048, 128], [1, 1024]])),
                c_xs, mark=f"xsld_{k}")
            P.w("scalar", c_t, f"dnmm_{k}")
            if k >= 2:
                P.w("scalar", c_v, f"comb_{k-2}")
            P.do("scalar", lambda e, hc=hc, k=k: e.activation(
                out=gt[k % 2][:], in_=pdn[hc % 2][:], func=AF.Silu, scale=wsdp_eff[:, hc:hc + 1]),
                c_s, mark=f"sil_{k}")
            P.w("vector", c_s, f"sil_{k}")
            P.w("vector", c_xs, f"xsld_{k}")
            if k >= 2:
                P.w("vector", c_ost, f"ost_{k-2}")
            P.do("vector", lambda e, hc=hc, k=k: e.tensor_tensor(
                out=ot3[hc % 2][:], in0=st3[hc % 2][:], in1=gt[k % 2][:], op=OP.mult), c_v)
            P.do("vector", lambda e, hc=hc: e.tensor_tensor(
                out=ot3[hc % 2][:], in0=ot3[hc % 2][:], in1=xt3[hc % 2][:], op=OP.add),
                c_v, mark=f"comb_{k}")
            P.w("sync", c_v, f"comb_{k}")
            P.dma("sync", lambda e, hc=hc, tb=tb: e.dma_start(
                out=bass.AP(out_d.tensor, hc * 262144 + tb * 1024, [[2048, 128], [1, 1024]]),
                in_=ot3[hc % 2][:]), c_ost, mark=f"ost_{tb*16+hc}")

        for tb in range(2):
            if tb == 0:
                P.w("sync", c_vbst, f"vbst_{NI-1}")
                P.w("sync", c_sst, "sst_63")
                P.w("tensor", c_v, "p2_v_done")
                P.w("tensor", c_s, "p2_s_done")
            else:
                P.w("sync", c_t, f"dnmm_{(tb-1)*16+15}")
            for grp in range(4):
                for rr in range(8 * grp, 8 * grp + 8):
                    emit_pb_chunk(tb, rr)
                emit_dn_grp(tb, 0, grp, chase=True)
            emit_dn_epi(tb, 0)
            for hc in range(1, 16):
                for grp in range(4):
                    emit_dn_grp(tb, hc, grp, chase=False)
                emit_dn_epi(tb, hc)
        P.w("sync", c_ost, val=16 * 32)

    p0()
    p1a()
    ps1.close()
    ps1b = contextlib.ExitStack()
    prep1 = ps1b.enter_context(nc.psum_tensor("prep1a", [128, 512], F32))
    pw1 = ps1b.enter_context(nc.psum_tensor("pw1", [96, 1], F32))
    p1a_epi(prep1, pw1)
    ps1b.close()
    ps1c = contextlib.ExitStack()
    ps_xd = ps1c.enter_context(nc.psum_tensor("ps_xd", [96, 2048], F32))
    ppa = [ps1c.enter_context(nc.psum_tensor(f"ppa{i}", [128, 512], F32)) for i in range(2)]
    p1b(ps_xd, ppa)
    p1b_epi(ps_xd, ppa)
    ps1c.close()
    ps1d = contextlib.ExitStack()
    pxh = [ps1d.enter_context(nc.psum_tensor(f"pxh{i}", [128, 512], F32)) for i in range(2)]
    p1c(pxh)
    ps1d.close()

    p1sc.close()
    for eng in ("sync", "gpsimd"):
        P.w(eng, c_t, "p1c_t_done")
        P.w(eng, c_s, "p1c_s_done")
        P.w(eng, c_v, "xdbl_v")
        P.w(eng, c_xnst, val=16 * NDC)
        P.w(eng, c_xbst, val=16 * 8 * NDC)

    p2sc = contextlib.ExitStack()
    sb2 = lambda n, s, dt: p2sc.enter_context(nc.sbuf_tensor(n, s, dt))
    wub = [sb2(f"wub{i}", [128, 2048], BF16) for i in range(2)]
    wgb = [sb2(f"wgb{i}", [128, 2048], BF16) for i in range(2)]
    ub = sb2("ub", [128, 2048], BF16)
    sgb = sb2("sgb", [128, 2048], BF16)
    vtile = sb2("vtile", [128, 2048], BF16)
    vA = vtile
    xnf2 = [sb2(f"xnf2_{i}", [128, 2048], BF16) for i in range(2)]
    nlsp = [sb2(f"nlsp{i}", [128, 2048], BF16) for i in range(2)]
    _nx = sb2("ndx0", [128, 2048], BF16)
    ndx = [_nx, _nx]
    dAh = [sb2(f"dAh{i}", [128, 1024], F32) for i in range(3)]
    _bx = sb2("BXh0", [128, 1024], BF16)
    BXh = [_bx, _bx, _bx]
    hCs = [sb2(f"hCs{i}", [128, 1024], BF16) for i in range(16)]
    stcol = sb2("stcol", [128, 16], F32)
    sigt = sb2("sigt", [128, 512], F32)
    _ss = sb2("sst0", [128, 512], F32)
    sst = [_ss, _ss]
    ps2 = contextlib.ExitStack()
    pu = [ps2.enter_context(nc.psum_tensor(f"pu{i}", [128, 512], F32)) for i in range(2)]
    pg0 = ps2.enter_context(nc.psum_tensor("pg0", [128, 512], F32))
    pg = [pg0, pg0]
    ppa2 = ps2.enter_context(nc.psum_tensor("ppa2", [128, 512], F32))
    prep = [ps2.enter_context(nc.psum_tensor(f"prep{i}", [128, 512], F32)) for i in range(2)]
    prep3 = ps2.enter_context(nc.psum_tensor("prep3", [128, 512], F32))
    py = ps2.enter_context(nc.psum_tensor("py", [128, 512], F32))
    build_p2((wub, wgb, ub, sgb, vtile, vA, xnf2, nlsp, ndx, dAh, BXh,
              hCs, stcol, sigt, sst, pu, pg, ppa2, prep, prep3, py))
    ps2.close()

    p2sc.close()
    xhsc.close()
    p12.close()
    for eng in ("sync", "gpsimd"):
        P.w(eng, c_t, "p2_t_done")
        P.w(eng, c_s, "p2_s_done")
        P.w(eng, c_v, "p2_v_done")
        P.w(eng, c_sst, val=16 * 64)
        P.w(eng, c_vbst, val=16 * 2 * NI)

    p3sc = contextlib.ExitStack()
    sb3 = lambda n, s, dt: p3sc.enter_context(nc.sbuf_tensor(n, s, dt))
    vhat = sb3("vhat", [128, 64 * 1024], BF16)
    vbst_t = [sb3(f"vbst{i}", [128, 2048], BF16) for i in range(2)]
    wdb = [sb3(f"wdb{i}", [128, 2048], BF16) for i in range(2)]
    gt = [sb3(f"gt{i}", [128, 1024], F32) for i in range(2)]
    xt3 = [sb3(f"xt3_{i}", [128, 1024], F32) for i in range(2)]
    st3 = [sb3(f"st3_{i}", [128, 1024], F32) for i in range(2)]
    ot3 = [sb3(f"ot3_{i}", [128, 1024], F32) for i in range(2)]
    ps3 = contextlib.ExitStack()
    ppb = [ps3.enter_context(nc.psum_tensor(f"ppb{i}", [128, 1024], F32)) for i in range(2)]
    pdn = [ps3.enter_context(nc.psum_tensor(f"pdn{i}", [128, 1024], F32)) for i in range(2)]
    build_p3((vhat, vbst_t, wdb, gt, xt3, st3, ot3, ppb, pdn))
    ps3.close()
    p3sc.close()

    with nc.Block() as blk:
        blk.sync(lambda e: [op(e) for op in P.ops["sync"]])
        blk.gpsimd(lambda e: [op(e) for op in P.ops["gpsimd"]])
        blk.tensor(lambda e: [op(e) for op in P.ops["tensor"]])
        blk.scalar(lambda e: [op(e) for op in P.ops["scalar"]])
        blk.vector(lambda e: [op(e) for op in P.ops["vector"]])
    es.close()
    build_program._last_P = P
    return nc


def kernel(**inputs):
    lay = host_layouts(inputs)
    cst = host_consts()
    x = np.asarray(inputs["L0_attn_x"], np.float32)
    nc = build_program()
    base = {}
    base.update(lay)
    base.update(cst)
    in_maps = []
    for b in range(8):
        m = dict(base)
        m["xT"] = np.ascontiguousarray(x[b].T)
        in_maps.append(m)
    res = run_bass_kernel_spmd(nc, in_maps, core_ids=list(range(8)))
    kernel._last_res = res
    out = np.empty((8, L, D), np.float32)
    for b in range(8):
        out[b] = res.results[b]["out"].T
    return out


# version 33
# speedup vs baseline: 1.0560x; 1.0560x over previous
import numpy as np
import ml_dtypes

import concourse.bass as bass
import concourse.mybir as mybir
from concourse.alu_op_type import AluOpType as OP
from concourse.bass_utils import run_bass_kernel_spmd

F32 = mybir.dt.float32
BF16 = mybir.dt.bfloat16
AF = mybir.ActivationFunctionType
BF = ml_dtypes.bfloat16

D = 2048
L = 2048
NDC = 16
NI = 64
ENGS = ("sync", "gpsimd", "tensor", "scalar", "vector")

DEBUG_TAPS = False


class Ctr:
    def __init__(self, sem):
        self.sem = sem
        self.n = 0


class Prog:

    def __init__(self):
        self.ops = {k: [] for k in ENGS}
        self.marks = {}
        self.trace = {k: [] for k in ENGS}

    def w(self, eng, ctr, key=None, val=None, opt=False):
        sem = ctr.sem
        self.trace[eng].append(("wait", id(ctr), key, val, opt))
        if key is not None:
            if opt:
                self.ops[eng].append(
                    lambda e: e.wait_ge(sem, self.marks[key]) if key in self.marks else None)
            else:
                self.ops[eng].append(lambda e: e.wait_ge(sem, self.marks[key]))
        else:
            self.ops[eng].append(lambda e: e.wait_ge(sem, val))

    def do(self, eng, fn, ctr=None, inc=1, mark=None):
        if ctr is not None:
            ctr.n += inc
            if mark is not None:
                self.marks[mark] = ctr.n
            self.trace[eng].append(("inc", id(ctr), inc, mark))
            self.ops[eng].append(lambda e: fn(e).then_inc(ctr.sem, inc))
        else:
            self.trace[eng].append(("op", None, 0, None))
            self.ops[eng].append(fn)

    def dma(self, eng, fn, ctr, mark=None):
        self.do(eng, fn, ctr, 16, mark)

    def mk(self, key, ctr):
        self.marks[key] = ctr.n


def _hadamard(n):
    i = np.arange(n)
    a = i[:, None] & i[None, :]
    par = np.zeros((n, n), np.int64)
    while a.any():
        par ^= a & 1
        a >>= 1
    return (1.0 - 2.0 * par).astype(np.float32)


def host_consts():
    c = {}
    H16 = _hadamard(16)
    H64 = _hadamard(64)
    c["h128f"] = _hadamard(128).astype(np.float32)
    c["h128b"] = _hadamard(128).astype(BF)
    p = np.arange(128)
    c["s1b"] = ((p[:, None] >> 4 == p[None, :] >> 4) * H16[np.ix_(p & 15, p & 15)]).astype(BF)
    c["s2b"] = ((p[:, None] >> 6 == p[None, :] >> 6) * H64[np.ix_(p & 63, p & 63)]).astype(BF)
    Rp = np.zeros((16, 128, 128), np.float32)
    Sj = np.zeros((16, 128, 128), np.float32)
    for j in range(16):
        Rp[j, 8 * j + (p >> 4), p] = 1.0
        Sj[j, p, 8 * j + (p >> 4)] = 1.0
    c["rpb"] = np.ascontiguousarray(Rp.transpose(1, 0, 2).reshape(128, 2048)).astype(BF)
    c["sjb"] = np.ascontiguousarray(Sj.transpose(1, 0, 2).reshape(128, 2048)).astype(BF)
    REPB = np.zeros((32, 128), np.float32)
    REPC = np.zeros((32, 128), np.float32)
    REPB[p & 15, p] = 1.0
    REPC[16 + (p & 15), p] = 1.0
    c["repbb"] = REPB.astype(BF)
    c["repcb"] = REPC.astype(BF)
    c["ones128b"] = np.ones((128, 1), BF)
    c["onesk1f"] = np.ones((1, 128), np.float32)
    return c


def host_layouts(inp):
    d = {}
    Wxp = np.asarray(inp["L0_attn_W_xproj"], np.float32)
    Wxp_r = np.concatenate([Wxp[64:80], Wxp[80:96], Wxp[0:64]], axis=0)
    d["wxp"] = np.ascontiguousarray(
        Wxp_r.reshape(96, 16, 128).transpose(2, 1, 0).reshape(128, 16 * 96))
    d["wdt"] = np.ascontiguousarray(np.asarray(inp["L0_attn_W_dt"], np.float32).T)
    d["bdt"] = np.ascontiguousarray(np.asarray(inp["L0_attn_b_dt"], np.float32).reshape(16, 128).T)
    d["dvec"] = np.ascontiguousarray(np.asarray(inp["L0_attn_D"], np.float32).reshape(16, 128).T)
    Alog = np.asarray(inp["L0_attn_A_log"], np.float32)
    pidx = np.arange(128)
    A_dev = np.empty((128, 256), np.float32)
    for dc in range(16):
        for j in range(16):
            A_dev[:, dc * 16 + j] = Alog[dc * 128 + 8 * j + (pidx >> 4), pidx & 15]
    d["adev"] = A_dev
    d["sug"] = np.ascontiguousarray(np.asarray(inp["L0_ffn_SU_ug"], np.float32).reshape(16, 128).T)
    ws = np.asarray(inp["L0_ffn_Wscale_ug"], np.float32)
    d["wsug"] = np.ascontiguousarray(
        np.concatenate([ws[:8192].reshape(64, 128).T, ws[8192:].reshape(64, 128).T], axis=1))
    d["sudp"] = np.ascontiguousarray(np.asarray(inp["L0_ffn_SU_dp"], np.float32).reshape(64, 128).T)
    d["wsdp"] = np.ascontiguousarray(np.asarray(inp["L0_ffn_Wscale_dp"], np.float32).reshape(16, 128).T)
    q = np.arange(128)
    r1 = np.arange(16)
    f1 = ((q[:, None] & 15) << 7) | ((q[:, None] >> 4) << 4) | r1[None, :]
    for name, key in (("L0_ffn_W_up", "wu"), ("L0_ffn_W_gate", "wg")):
        W = np.asarray(inp[name], np.float32).reshape(64, 128, 2048)
        d[key] = np.ascontiguousarray(W[:, :, f1].transpose(0, 2, 3, 1).reshape(64, 128, 2048))
    q2 = np.arange(128)
    r2 = np.arange(64)
    i2 = ((q2[:, None] & 63) << 7) | ((q2[:, None] >> 6) << 6) | r2[None, :]
    Wd = np.asarray(inp["L0_ffn_W_down"], np.float32).reshape(16, 128, 8192)
    d["wd"] = np.ascontiguousarray(Wd[:, :, i2].transpose(0, 2, 3, 1).reshape(16, 128, 8192))
    return d


def build_program():
    nc = bass.Bass("TRN2", target_bir_lowering=False)
    P = Prog()

    dram_in = lambda n, s, dt=F32: nc.dram_tensor(n, s, dt, kind="ExternalInput").ap()
    xT = dram_in("xT", [D, L])
    wxp = dram_in("wxp", [128, 16 * 96])
    wdt_d = dram_in("wdt", [64, 2048])
    bdt_d = dram_in("bdt", [128, 16])
    dvec_d = dram_in("dvec", [128, 16])
    adev_d = dram_in("adev", [128, 256])
    sug_d = dram_in("sug", [128, 16])
    wsug_d = dram_in("wsug", [128, 128])
    sudp_d = dram_in("sudp", [128, 64])
    wsdp_d = dram_in("wsdp", [128, 16])
    wu_d = dram_in("wu", [64, 128, 2048])
    wg_d = dram_in("wg", [64, 128, 2048])
    wd_d = dram_in("wd", [16, 128, 8192])
    h128f_d = dram_in("h128f", [128, 128])
    h128b_d = dram_in("h128b", [128, 128], BF16)
    s1b_d = dram_in("s1b", [128, 128], BF16)
    s2b_d = dram_in("s2b", [128, 128], BF16)
    rpb_d = dram_in("rpb", [128, 2048], BF16)
    sjb_d = dram_in("sjb", [128, 2048], BF16)
    repbb_d = dram_in("repbb", [32, 128], BF16)
    repcb_d = dram_in("repcb", [32, 128], BF16)
    ones128b_d = dram_in("ones128b", [128, 1], BF16)
    onesk1f_d = dram_in("onesk1f", [1, 128])

    _ik = "ExternalOutput" if DEBUG_TAPS else "Internal"
    if DEBUG_TAPS:
        dbg_brep = nc.dram_tensor("dbg_brep", [128, 2048], BF16, kind="ExternalOutput").ap()
        dbg_crep = nc.dram_tensor("dbg_crep", [128, 2048], BF16, kind="ExternalOutput").ap()
        dbg_dpre = nc.dram_tensor("dbg_dpre", [64, 2048], BF16, kind="ExternalOutput").ap()
        dbg_nlsp = nc.dram_tensor("dbg_nlsp", [128, 2048], BF16, kind="ExternalOutput").ap()
        dbg_ndx = nc.dram_tensor("dbg_ndx", [128, 2048], BF16, kind="ExternalOutput").ap()
        dbg_dA = nc.dram_tensor("dbg_dA", [128, 1024], F32, kind="ExternalOutput").ap()
        dbg_BX = nc.dram_tensor("dbg_BX", [128, 1024], BF16, kind="ExternalOutput").ap()
        dbg_h = nc.dram_tensor("dbg_h", [128, 1024], BF16, kind="ExternalOutput").ap()
        dbg_vhat = nc.dram_tensor("dbg_vhat", [128, 64 * 1024], BF16, kind="ExternalOutput").ap()
    xn_hbm = nc.dram_tensor("xn_hbm", [D, L], F32, kind=_ik).ap()
    s_hbm = nc.dram_tensor("s_hbm", [D, L], F32, kind=_ik).ap()
    xb2 = nc.dram_tensor("xb2", [128, 16 * 2048], BF16, kind=_ik).ap()
    vb2 = nc.dram_tensor("vb2", [128, 64 * 2048], BF16, kind=_ik).ap()
    out_d = nc.dram_tensor("out", [D, L], F32, kind="ExternalOutput").ap()

    import contextlib
    es = contextlib.ExitStack()
    sb = lambda n, s, dt: es.enter_context(nc.sbuf_tensor(n, s, dt))
    mksem = lambda n: Ctr(es.enter_context(nc.semaphore(n)))

    s2b = es.enter_context(nc.sbuf_tensor("s2b_s", [128, 128], BF16))
    wsdp_eff = es.enter_context(nc.sbuf_tensor("wsdp_eff", [128, 16], F32))
    import contextlib as _ctl
    p12 = _ctl.ExitStack()
    sb = lambda n, s, dt: p12.enter_context(nc.sbuf_tensor(n, s, dt))
    wxpb = sb("wxpb", [128, 1536], BF16)
    wdtb = sb("wdtb", [64, 2048], BF16)
    bdtn = sb("bdtn", [128, 16], F32)
    dcolv = sb("dcolv", [128, 16], F32)
    posA = sb("posA", [128, 256], F32)
    h128su = sb("h128su", [128, 2048], BF16)
    h128b = sb("h128b_s", [128, 128], BF16)
    s1b = sb("s1b_s", [128, 128], BF16)
    rpb = sb("rpb_s", [128, 2048], BF16)
    sjb = sb("sjb_s", [128, 2048], BF16)
    repbb = sb("repbb_s", [32, 128], BF16)
    repcb = sb("repcb_s", [32, 128], BF16)
    ones128b = sb("ones128b_s", [128, 1], BF16)
    onesk1f = sb("onesk1f_s", [1, 128], F32)
    wsu_eff = sb("wsu_eff", [128, 64], F32)
    wsg_eff = sb("wsg_eff", [128, 64], F32)
    su1c = sb("su1c", [128, 16], F32)
    w1col = sb("w1col", [96, 1], F32)
    BC32 = sb("BC32", [32, 2048], BF16)
    dpreb = sb("dpreb", [64, 2048], BF16)
    B_rep = sb("B_rep", [128, 2048], BF16)
    C_rep = sb("C_rep", [128, 2048], BF16)

    c_ld = mksem("c_ld")
    c_x = mksem("c_x")
    c_t = mksem("c_t")
    c_s = mksem("c_s")
    c_v = mksem("c_v")
    c_g = mksem("c_g")
    c_xnst = mksem("c_xnst")
    c_xbst = mksem("c_xbst")
    c_xbld = mksem("c_xbld")
    c_w = mksem("c_w")
    c_xnld = mksem("c_xnld")
    c_vbst = mksem("c_vbst")
    c_sst = mksem("c_sst")
    c_vbld = mksem("c_vbld")
    c_wd = mksem("c_wd")
    c_xs = mksem("c_xs")
    c_ost = mksem("c_ost")
    c_ph = mksem("c_ph")
    c_dbg = mksem("c_dbg")

    M = P.marks

    def p0():
        loads = [
            (bdtn, bdt_d), (dcolv, dvec_d), (posA, adev_d), (su1c, sug_d),
            (h128b, h128b_d), (s1b, s1b_d), (s2b, s2b_d), (rpb, rpb_d),
            (sjb, sjb_d), (repbb, repbb_d), (repcb, repcb_d),
            (ones128b, ones128b_d), (onesk1f, onesk1f_d), (sudp_s, sudp_d),
            (h128f_s, h128f_d), (wdtb if False else None, None),
        ]
        for dst, src in loads:
            if dst is None:
                continue
            P.dma("sync", lambda e, d=dst, s=src: e.dma_start(out=d[:], in_=s), c_ld)
        P.dma("sync", lambda e: e.dma_start(out=wsu_eff[:], in_=bass.AP(wsug_d.tensor, 0, [[128, 128], [1, 64]])), c_ld)
        P.dma("sync", lambda e: e.dma_start(out=wsg_eff[:], in_=bass.AP(wsug_d.tensor, 64, [[128, 128], [1, 64]])), c_ld)
        P.dma("sync", lambda e: e.dma_start(out=wsdp_eff[:], in_=wsdp_d), c_ld)
        P.dma("sync", lambda e: e.dma_start(out=murep[:, 0:1536], in_=wxp), c_ld)
        P.dma("sync", lambda e: e.dma_start(out=xdblt[0:64, :], in_=wdt_d), c_ld)
        P.mk("p0_ld", c_ld)

        P.w("scalar", c_ld, "p0_ld")
        P.do("scalar", lambda e: e.activation(out=posA[:], in_=posA[:], func=AF.Exp), c_s)
        P.do("scalar", lambda e: e.activation(out=posA[:], in_=posA[:], func=AF.Copy, scale=-1.0), c_s)
        P.do("scalar", lambda e: e.activation(out=su1c[:], in_=su1c[:], func=AF.Copy,
                                              scale=float(1.0 / (64.0 * np.sqrt(2048.0)))), c_s)
        P.do("scalar", lambda e: e.activation(out=wsg_eff[:], in_=wsg_eff[:], func=AF.Copy, scale=64.0), c_s)
        P.do("scalar", lambda e: e.activation(out=wsdp_eff[:], in_=wsdp_eff[:], func=AF.Copy, scale=64.0), c_s)
        P.do("scalar", lambda e: e.copy(out=wxpb[:], in_=murep[:, 0:1536]), c_s)
        P.do("scalar", lambda e: e.copy(out=wdtb[:], in_=xdblt[0:64, :]), c_s, mark="p0_s")

        P.w("vector", c_s, "p0_s")
        P.do("vector", lambda e: e.tensor_tensor(out=wsu_eff[:], in0=wsu_eff[:], in1=sudp_s[:], op=OP.mult), c_v)
        P.do("vector", lambda e: e.tensor_scalar_mul(out=wsu_eff[:], in0=wsu_eff[:],
                                                     scalar1=float(1.0 / np.sqrt(8192.0))), c_v)
        for fc in range(16):
            P.do("vector", lambda e, fc=fc: e.tensor_scalar_mul(
                out=h128su[:, fc * 128:(fc + 1) * 128], in0=h128f_s[:],
                scalar1=su1c[:, fc:fc + 1]), c_v, mark="p0_v")

    xhsc = contextlib.ExitStack()
    xhat = xhsc.enter_context(nc.sbuf_tensor("xhat", [128, 16 * 2048], BF16))
    p1sc = contextlib.ExitStack()
    sb1 = lambda n, s, dt: p1sc.enter_context(nc.sbuf_tensor(n, s, dt))
    sudp_s = p1sc.enter_context(nc.sbuf_tensor("sudp_s", [128, 64], F32))
    h128f_s = p1sc.enter_context(nc.sbuf_tensor("h128f_s", [128, 128], F32))
    murep = sb1("murep", [128, 2048], F32)
    rrep = sb1("rrep", [128, 2048], F32)
    xdblt = sb1("xdblt", [96, 2048], F32)
    tmp96 = sb1("tmp96", [96, 2048], F32)
    xt = [sb1(f"xt{i}", [128, 2048], F32) for i in range(2)]
    xbt = [sb1(f"xbt{i}", [128, 2048], BF16) for i in range(2)]
    xsq = [sb1(f"xsq{i}", [128, 2048], BF16) for i in range(2)]
    xnf = [sb1(f"xnf{i}", [128, 2048], F32) for i in range(2)]
    xpa = sb1("xpa", [128, 2048], BF16)
    xbst = [sb1(f"xbst{i}", [128, 4096], BF16) for i in range(2)]

    ps1 = contextlib.ExitStack()
    ps_sum = ps1.enter_context(nc.psum_tensor("ps_sum", [1, 2048], F32))
    ps_sq = ps1.enter_context(nc.psum_tensor("ps_sq", [1, 2048], F32))

    def p1a():
        for dc in range(NDC):
            b = dc % 2
            if dc >= 2:
                P.w("sync", c_s, f"p1a_s_{dc-2}")
                P.w("sync", c_v, f"p1a_v_{dc-2}")
            P.dma("sync", lambda e, b=b, dc=dc: e.dma_start(out=xt[b][:], in_=xT[dc * 128:(dc + 1) * 128, :]),
                  c_x, mark=f"p1a_x_{dc}")
            P.w("scalar", c_x, f"p1a_x_{dc}")
            if dc >= 2:
                P.w("scalar", c_t, f"p1a_t_{dc-2}")
            P.do("scalar", lambda e, b=b: e.copy(out=xbt[b][:], in_=xt[b][:]), c_s, mark=f"p1a_s_{dc}")
            P.w("vector", c_x, f"p1a_x_{dc}")
            if dc >= 2:
                P.w("vector", c_t, f"p1a_t_{dc-2}")
            P.do("vector", lambda e, b=b: e.tensor_tensor(out=xsq[b][:], in0=xt[b][:], in1=xt[b][:], op=OP.mult),
                 c_v, mark=f"p1a_v_{dc}")
            P.w("tensor", c_s, f"p1a_s_{dc}")
            P.w("tensor", c_v, f"p1a_v_{dc}")
            for c in range(4):
                sl = slice(c * 512, (c + 1) * 512)
                P.do("tensor", lambda e, b=b, dc=dc, sl=sl: e.matmul(
                    ps_sum[0:1, sl], ones128b[:], xbt[b][:, sl],
                    start=(dc == 0), stop=(dc == NDC - 1)), c_t)
                P.do("tensor", lambda e, b=b, dc=dc, sl=sl: e.matmul(
                    ps_sq[0:1, sl], ones128b[:], xsq[b][:, sl],
                    start=(dc == 0), stop=(dc == NDC - 1)), c_t,
                    mark=f"p1a_t_{dc}" if c == 3 else None)

    def p1a_epi(prep1, pw1):
        P.w("vector", c_t, "p1a_t_15")
        P.do("vector", lambda e: e.tensor_scalar_mul(out=tmp96[0:1, :], in0=ps_sum[0:1, :], scalar1=float(1.0 / D)), c_v)
        P.do("vector", lambda e: e.tensor_scalar_mul(out=xdblt[0:1, :], in0=ps_sq[0:1, :], scalar1=float(1.0 / D)),
             c_v, mark="rows_v")
        P.w("scalar", c_v, "rows_v")
        P.do("scalar", lambda e: e.activation(out=murep[0:1, :], in_=tmp96[0:1, :], func=AF.Square), c_s, mark="musq_s")
        P.w("vector", c_s, "musq_s")
        P.do("vector", lambda e: e.tensor_tensor(out=xdblt[0:1, :], in0=xdblt[0:1, :], in1=murep[0:1, :], op=OP.subtract), c_v)
        P.do("vector", lambda e: e.tensor_scalar_add(out=xdblt[0:1, :], in0=xdblt[0:1, :], scalar1=1e-5),
             c_v, mark="var_v")
        P.w("scalar", c_v, "var_v")
        P.do("scalar", lambda e: e.activation(out=xdblt[0:1, :], in_=xdblt[0:1, :], func=AF.Sqrt),
             c_s, mark="sqrt_s")
        P.w("vector", c_s, "sqrt_s")
        P.do("vector", lambda e: e.reciprocal(out=xdblt[0:1, :], in_=xdblt[0:1, :]), c_v, mark="rrow_v")
        P.w("tensor", c_v, "rrow_v")
        k = 0
        for dst, rowsrc in ((murep, tmp96), (rrep, xdblt)):
            for c in range(4):
                sl = slice(c * 512, (c + 1) * 512)
                if k >= 1:
                    P.w("tensor", c_s, f"repev_{k-1}")
                P.do("tensor", lambda e, rowsrc=rowsrc, sl=sl: e.matmul(
                    prep1[:], onesk1f[:], rowsrc[0:1, sl], start=True, stop=True),
                    c_t, mark=f"repmm_{k}")
                P.w("scalar", c_t, f"repmm_{k}")
                P.do("scalar", lambda e, dst=dst, sl=sl: e.copy(out=dst[:, sl], in_=prep1[:]),
                     c_s, mark=f"repev_{k}")
                k += 1


    def p1b(ps_xd, ppa):
        for dc in range(NDC):
            b = dc % 2
            if dc < 2:
                P.w("sync", c_s, "p1a_s_15")
                P.w("sync", c_v, "p1a_v_15")
            if dc >= 2:
                P.w("sync", c_v, f"p1b_xn_{dc-2}")
            P.dma("sync", lambda e, b=b, dc=dc: e.dma_start(out=xt[b][:], in_=xT[dc * 128:(dc + 1) * 128, :]),
                  c_x, mark=f"p1b_x_{dc}")
            P.w("vector", c_x, f"p1b_x_{dc}")
            if dc == 0:
                P.w("vector", c_s, "repev_7")
            if dc >= 2:
                P.w("vector", c_xnst, f"p1b_xnst_{dc-2}")
                P.w("vector", c_s, f"p1b_cast_{dc-2}")
            P.do("vector", lambda e, b=b: e.tensor_tensor(out=xnf[b][:], in0=xt[b][:], in1=murep[:], op=OP.subtract), c_v)
            P.do("vector", lambda e, b=b: e.tensor_tensor(out=xnf[b][:], in0=xnf[b][:], in1=rrep[:], op=OP.mult),
                 c_v, mark=f"p1b_xn_{dc}")
            P.w("gpsimd", c_v, f"p1b_xn_{dc}")
            P.dma("gpsimd", lambda e, b=b, dc=dc: e.dma_start(out=xn_hbm[dc * 128:(dc + 1) * 128, :], in_=xnf[b][:]),
                  c_xnst, mark=f"p1b_xnst_{dc}")
            P.w("scalar", c_v, f"p1b_xn_{dc}")
            if dc >= 2:
                P.w("scalar", c_t, f"p1b_t_{dc-2}")
            P.do("scalar", lambda e, b=b: e.copy(out=xbt[b][:], in_=xnf[b][:]), c_s, mark=f"p1b_cast_{dc}")
            P.w("tensor", c_s, f"p1b_cast_{dc}")
            for c in range(4):
                sl = slice(c * 512, (c + 1) * 512)
                P.do("tensor", lambda e, b=b, dc=dc, sl=sl: e.matmul(
                    ps_xd[0:96, sl], wxpb[:, dc * 96:(dc + 1) * 96], xbt[b][:, sl],
                    start=(dc == 0), stop=(dc == NDC - 1)), c_t)
            for c in range(4):
                sl = slice(c * 512, (c + 1) * 512)
                k = dc * 4 + c
                if k >= 2:
                    P.w("tensor", c_s, f"p1b_pa_{k-2}")
                P.do("tensor", lambda e, b=b, dc=dc, sl=sl, c=c: e.matmul(
                    ppa[c % 2][:], h128su[:, dc * 128:(dc + 1) * 128], xbt[b][:, sl],
                    start=True, stop=True), c_t, mark=f"p1b_pamm_{k}" + ("" if c < 3 else "_last"))
                if c == 3:
                    P.mk(f"p1b_t_{dc}", c_t)
                P.w("scalar", c_t, f"p1b_pamm_{k}" + ("" if c < 3 else "_last"))
                if dc >= 1 and c == 0:
                    P.w("scalar", c_xbst, f"p1b_xbst_{dc-1}")
                P.do("scalar", lambda e, sl=sl, c=c: e.copy(out=xpa[:, sl], in_=ppa[c % 2][:]),
                     c_s, mark=f"p1b_pa_{k}")
            P.w("gpsimd", c_s, f"p1b_pa_{dc*4+3}")
            for g in range(8):
                P.dma("gpsimd", lambda e, g=g, dc=dc: e.dma_start(
                    out=bass.AP(xb2.tensor, (g * 16 + dc) * 16 * 2048, [[2048, 16], [1, 2048]]),
                    in_=xpa[g * 16:(g + 1) * 16, :]),
                    c_xbst, mark=f"p1b_xbst_{dc}" if g == 7 else None)

    def p1b_epi(ps_xd, ppa):
        P.w("vector", c_t, "p1b_t_15")
        P.do("vector", lambda e: e.tensor_copy(xdblt[:], ps_xd[0:96, :]), c_v, mark="xdbl_v")
        P.w("scalar", c_v, "xdbl_v")
        P.do("scalar", lambda e: e.copy(out=BC32[:], in_=xdblt[0:32, :]), c_s)
        P.do("scalar", lambda e: e.copy(out=dpreb[0:32, :], in_=xdblt[32:64, :]), c_s)
        P.do("scalar", lambda e: e.copy(out=dpreb[32:64, :], in_=xdblt[64:96, :]), c_s, mark="bc_s")
        P.w("tensor", c_s, "bc_s")
        k = 0
        for st, dst in ((repbb, B_rep), (repcb, C_rep)):
            for c in range(4):
                sl = slice(c * 512, (c + 1) * 512)
                if k >= 2:
                    P.w("tensor", c_s, f"bcrep_s_{k-2}")
                P.do("tensor", lambda e, st=st, sl=sl, k=k: e.matmul(
                    ppa[k % 2][:], st[:], BC32[:, sl], start=True, stop=True), c_t, mark=f"bcrep_t_{k}")
                P.w("scalar", c_t, f"bcrep_t_{k}")
                P.do("scalar", lambda e, dst=dst, sl=sl, k=k: e.copy(out=dst[:, sl], in_=ppa[k % 2][:]),
                     c_s, mark=f"bcrep_s_{k}")
                k += 1

    def p1c(pxh):
        if DEBUG_TAPS:
            P.w("sync", c_s, "bcrep_s_7")
            P.dma("sync", lambda e: e.dma_start(out=dbg_brep[:], in_=B_rep[:]), c_dbg)
            P.dma("sync", lambda e: e.dma_start(out=dbg_crep[:], in_=C_rep[:]), c_dbg)
            P.dma("sync", lambda e: e.dma_start(out=dbg_dpre[:], in_=dpreb[:]), c_dbg)
        P.w("sync", c_xbst, f"p1b_xbst_{NDC-1}")
        P.w("tensor", c_s, "bcrep_s_7")
        P.w("tensor", c_v, "xdbl_v")
        for k in range(8):
            if k >= 2:
                P.w("sync", c_t, f"p1c_t_{k-2}")
            P.dma("sync", lambda e, k=k: e.dma_start(
                out=xbst[k % 2][:], in_=bass.AP(xb2.tensor, k * 4096, [[32768, 128], [1, 4096]])),
                c_xbld, mark=f"p1c_ld_{k}")
            P.w("tensor", c_xbld, f"p1c_ld_{k}")
            for c in range(8):
                nmm = 8 * k + c
                if nmm >= 2:
                    P.w("tensor", c_s, f"p1c_s_{nmm-2}")
                P.do("tensor", lambda e, k=k, c=c, nmm=nmm: e.matmul(
                    pxh[nmm % 2][:], s1b[:], xbst[k % 2][:, c * 512:(c + 1) * 512],
                    start=True, stop=True), c_t, mark=f"p1c_mm_{nmm}")
                if c == 7:
                    P.mk(f"p1c_t_{k}", c_t)
                P.w("scalar", c_t, f"p1c_mm_{nmm}")
                P.do("scalar", lambda e, nmm=nmm: e.copy(
                    out=xhat[:, nmm * 512:(nmm + 1) * 512], in_=pxh[nmm % 2][:]),
                    c_s, mark=f"p1c_s_{nmm}")
        P.mk("p1c_s_done", c_s)
        P.mk("p1c_t_done", c_t)
        P.mk("p1c_v_done", c_v)

    def build_p2(bufs):
        (wub, wgb, ub, sgb, vtile, vA, xnf2, nlsp, ndx, dAh, BXh, hCs,
         stcol, sigt, sst, pu, pg, ppa2, prep, prep3, py) = bufs
        prepd = [prep[0], ppa2]
        ppa3 = prep3
        prepx = [prep[1], py]

        def tok_Fw(hi):
            if hi >= 2:
                P.w("gpsimd", c_t, f"fdone_{hi-2}")
            P.dma("gpsimd", lambda e, hi=hi: e.dma_start(
                out=wub[hi % 2][:], in_=bass.AP(wu_d.tensor, hi * 262144, [[2048, 128], [1, 2048]])), c_w)
            P.dma("gpsimd", lambda e, hi=hi: e.dma_start(
                out=wgb[hi % 2][:], in_=bass.AP(wg_d.tensor, hi * 262144, [[2048, 128], [1, 2048]])),
                c_w, mark=f"w_ld_{hi}")

        def tok_Fa(hi, c, hf):
            wpar = hi % 2
            k = hi * 4 + c
            if hf == 0:
                if c == 0:
                    P.w("tensor", c_w, f"w_ld_{hi}")
                    if hi == 0:
                        P.w("tensor", c_s, "p1c_s_done")
                if k >= 2:
                    P.w("tensor", c_s, f"uev_{k-2}")
            def mm_up(e, wpar=wpar, c=c, hf=hf):
                for r1 in range(4 * hf, 4 * hf + 4):
                    inst = e.matmul(pu[c % 2][:], wub[wpar][:, r1 * 128:(r1 + 1) * 128],
                                    xhat[:, r1 * 2048 + c * 512: r1 * 2048 + (c + 1) * 512],
                                    start=(r1 == 0), stop=(r1 == 15))
                return inst
            if hf == 3:
                P.do("tensor", mm_up, c_t, mark=f"upmm_{k}")
            else:
                P.do("tensor", mm_up)

        def tok_Fb(hi, c, hf):
            wpar = hi % 2
            k = hi * 4 + c
            if hf == 0 and k >= 1:
                P.w("tensor", c_s, f"gev_{k-1}")
            def mm_g(e, wpar=wpar, c=c, hf=hf):
                for r1 in range(4 * hf, 4 * hf + 4):
                    inst = e.matmul(pg[c % 2][:], wgb[wpar][:, r1 * 128:(r1 + 1) * 128],
                                    xhat[:, r1 * 2048 + c * 512: r1 * 2048 + (c + 1) * 512],
                                    start=(r1 == 0), stop=(r1 == 15))
                return inst
            if hf == 3:
                P.do("tensor", mm_g, c_t, mark=f"gmm_{k}")
            else:
                P.do("tensor", mm_g)

        def tok_F(hi, c):
            k = hi * 4 + c
            sl = slice(c * 512, (c + 1) * 512)
            P.w("scalar", c_t, f"upmm_{k}")
            if k >= 4:
                P.w("scalar", c_v, f"vtt_{k-4}")
            P.do("scalar", lambda e, hi=hi, sl=sl, c=c: e.activation(
                out=ub[:, sl], in_=pu[c % 2][:], func=AF.Copy, scale=wsu_eff[:, hi:hi + 1]),
                c_s, mark=f"uev_{k}")
            P.w("scalar", c_t, f"gmm_{k}")
            P.do("scalar", lambda e, hi=hi, sl=sl, c=c: e.activation(
                out=sgb[:, sl], in_=pg[c % 2][:], func=AF.Copy, scale=wsg_eff[:, hi:hi + 1]),
                c_s, mark=f"gev_{k}")
            if c == 3:
                P.do("scalar", lambda e, hi=hi: e.activation(
                    out=sgb[:], in_=sgb[:], func=AF.Silu), c_s, mark=f"slu_{hi}")

        def tok_Fpa(hi, hp):
            for c in (2 * hp, 2 * hp + 1):
                k = hi * 4 + c
                sl = slice(c * 512, (c + 1) * 512)
                P.w("vector", c_s, f"slu_{hi}")
                if hi >= 1 and c == 0:
                    P.w("vector", c_t, f"pamm_{(hi-1)*4+3}")
                    P.w("vector", c_vbst, f"vbst_{hi-1}")
                P.do("vector", lambda e, sl=sl: e.tensor_tensor(out=vtile[:, sl], in0=sgb[:, sl], in1=ub[:, sl], op=OP.mult),
                     c_v, mark=f"vtt_{k}")
                P.w("tensor", c_v, f"vtt_{k}")
                if k >= 1:
                    P.w("tensor", c_s, f"paev_{k-1}")
                P.do("tensor", lambda e, sl=sl: e.matmul(ppa3[:], h128b[:], vtile[:, sl], start=True, stop=True),
                     c_t, mark=f"pamm_{k}")
                if c == 3:
                    P.mk(f"fdone_{hi}", c_t)
                P.w("scalar", c_t, f"pamm_{k}")
                if hi >= 1 and c == 0:
                    P.w("scalar", c_vbst, f"vbst_{hi-1}")
                P.do("scalar", lambda e, sl=sl: e.copy(out=vA[:, sl], in_=ppa3[:]), c_s, mark=f"paev_{k}")

        def tok_Fsync(hi):
            P.w("sync", c_s, f"paev_{hi*4+3}")
            for b6 in range(2):
                P.dma("sync", lambda e, b6=b6, hi=hi: e.dma_start(
                    out=bass.AP(vb2.tensor, (b6 * 64 + hi) * 131072, [[2048, 64], [1, 2048]]),
                    in_=vA[b6 * 64:(b6 + 1) * 64, :]),
                    c_vbst, mark=f"vbst_{hi}" if b6 == 1 else None)

        def tok_Mdp(dc, c):
            if dc == 0 and c == 0:
                P.w("tensor", c_s, "p1c_s_done")
                P.w("tensor", c_v, "xdbl_v")
            if c == 0:
                if dc >= 2:
                    P.w("gpsimd", c_v, f"sev_{(dc-2)*4+3}")
                P.w("gpsimd", c_xnst, f"p1b_xnst_{dc}")
                P.dma("gpsimd", lambda e, dc=dc: e.dma_start(
                    out=xnf2[dc % 2][:], in_=xn_hbm[dc * 128:(dc + 1) * 128, :]), c_xnld, mark=f"xnld_{dc}")
            if dc >= 1:
                P.w("tensor", c_s, f"dA_{64*dc - 2 + (c % 2)}")
            if dc * 4 + c >= 2:
                P.w("tensor", c_s, f"softp_{dc*4+c-2}")
            P.do("tensor", lambda e, dc=dc, c=c: e.matmul(
                prepd[c % 2][:], wdtb[:, dc * 128:(dc + 1) * 128],
                dpreb[:, c * 512:(c + 1) * 512], start=True, stop=True), c_t, mark=f"dproj_{dc*4+c}")
            P.w("scalar", c_t, f"dproj_{dc*4+c}")
            if dc >= 2 and c == 0:
                P.w("scalar", c_t, f"repd_{64*(dc-1)-1}")
                P.w("scalar", c_v, f"ndx_{dc-2}")
            P.do("scalar", lambda e, c=c, dc=dc: e.activation(
                out=sigt[:], in_=prepd[c % 2][:], func=AF.Exp,
                bias=bdtn[:, dc:dc + 1]), c_s)
            P.do("scalar", lambda e, c=c, dc=dc: e.activation(
                out=nlsp[dc % 2][:, c * 512:(c + 1) * 512], in_=sigt[:], func=AF.Ln, bias=1.0),
                c_s, mark=f"softp_{dc*4+c}")

        def tok_Msp(dc):
            P.w("vector", c_s, f"softp_{dc*4+3}")
            P.w("vector", c_xnld, f"xnld_{dc}")
            if dc >= 1:
                P.w("vector", c_t, f"repx_{64*dc-1}")
            P.do("vector", lambda e, dc=dc: e.tensor_tensor(
                out=ndx[dc % 2][:], in0=nlsp[dc % 2][:], in1=xnf2[dc % 2][:], op=OP.mult),
                c_v, mark=f"ndx_{dc}")

        def tok_Mtile(dc, half, j):
            ti = (dc * 2 + half) * 16 + j
            par = ti % 3
            for cc in range(2):
                g = ti * 2 + cc
                if g % 64 < 2:
                    P.w("tensor", c_s, f"softp_{dc*4+3}")
                if g >= 2:
                    P.w("tensor", c_s, f"dA_{g-2}")
                P.do("tensor", lambda e, j=j, half=half, cc=cc, g=g, dc=dc: e.matmul(
                    prepd[g % 2][:], rpb[:, j * 128:(j + 1) * 128],
                    nlsp[dc % 2][:, half * 1024 + cc * 512: half * 1024 + (cc + 1) * 512],
                    start=True, stop=True), c_t, mark=f"repd_{g}")
                if g % 64 == 0:
                    P.w("tensor", c_v, f"ndx_{dc}")
                if g % 32 == 1:
                    key = f"sev_{dc*4 + half*2 - 1}" if (dc, half) != (0, 0) else None
                    if key:
                        P.w("tensor", c_v, key)
                if g >= 2:
                    P.w("tensor", c_v, f"BX_{g-2}")
                P.do("tensor", lambda e, j=j, half=half, cc=cc, g=g, dc=dc: e.matmul(
                    prepx[g % 2][:], rpb[:, j * 128:(j + 1) * 128],
                    ndx[dc % 2][:, half * 1024 + cc * 512: half * 1024 + (cc + 1) * 512],
                    start=True, stop=True), c_t, mark=f"repx_{g}")
                P.w("scalar", c_t, f"repd_{g}")
                if ti >= 3 and cc == 0:
                    P.w("scalar", c_v, f"scan_{ti-3}")
                P.do("scalar", lambda e, par=par, cc=cc, dc=dc, j=j, g=g: e.activation(
                    out=dAh[par][:, cc * 512:(cc + 1) * 512], in_=prepd[g % 2][:], func=AF.Exp,
                    scale=posA[:, dc * 16 + j: dc * 16 + j + 1]), c_s, mark=f"dA_{g}")
                P.w("vector", c_t, f"repx_{g}")
                P.do("vector", lambda e, par=par, half=half, cc=cc, g=g: e.tensor_tensor(
                    out=BXh[par][:, cc * 512:(cc + 1) * 512], in0=prepx[g % 2][:],
                    in1=B_rep[:, half * 1024 + cc * 512: half * 1024 + (cc + 1) * 512], op=OP.mult),
                    c_v, mark=f"BX_{g}")
            if DEBUG_TAPS and ti == 0:
                P.w("sync", c_s, "dA_1")
                P.dma("sync", lambda e: e.dma_start(out=dbg_dA[:], in_=dAh[0][:]), c_dbg)
                P.w("sync", c_v, "BX_1")
                P.dma("sync", lambda e: e.dma_start(out=dbg_BX[:], in_=BXh[0][:]), c_dbg, mark="dbg_t0")
            P.w("vector", c_s, f"dA_{ti*2+1}")
            hi_half = dc * 2 + half
            if hi_half >= 1:
                P.w("vector", c_t, f"yfree_{hi_half-1}", opt=True)
            def scan_fn(e, par=par, half=half, j=j):
                init = 0.0 if half == 0 else stcol[:, j:j + 1]
                return e.tensor_tensor_scan(out=hCs[j][:], data0=dAh[par][:], data1=BXh[par][:],
                                            initial=init, op0=OP.mult, op1=OP.add)
            P.do("vector", scan_fn, c_v, mark=f"scan_{ti}")
            if DEBUG_TAPS and ti == 0:
                P.w("sync", c_v, "scan_0")
                P.dma("sync", lambda e: e.dma_start(out=dbg_h[:], in_=hCs[0][:]), c_dbg, mark="dbg_h0")
                P.w("vector", c_dbg, "dbg_h0")
            if DEBUG_TAPS and ti == 2:
                P.w("vector", c_dbg, "dbg_t0")
                P.w("scalar", c_dbg, "dbg_t0")
            if half == 0:
                P.do("vector", lambda e, j=j: e.tensor_copy(stcol[:, j:j + 1], hCs[j][:, 1023:1024]), c_v)
            P.do("vector", lambda e, half=half, j=j: e.tensor_tensor(
                out=hCs[j][:], in0=hCs[j][:], in1=C_rep[:, half * 1024:(half + 1) * 1024], op=OP.mult),
                c_v, mark=f"hC_{ti}")

        def tok_My(dc, half):
            hi_half = dc * 2 + half
            for q in range(2):
                qq = half * 2 + q
                kq = dc * 4 + qq
                if kq >= 1:
                    P.w("tensor", c_v, f"sev_{kq-1}", opt=True)
                if q == 0:
                    P.w("tensor", c_v, f"hC_{hi_half*16+15}")
                def mm_y(e, hi_half=hi_half, q=q):
                    for j in range(16):
                        inst = e.matmul(py[:], sjb[:, j * 128:(j + 1) * 128],
                                        hCs[j][:, q * 512:(q + 1) * 512],
                                        start=(j == 0), stop=(j == 15))
                    return inst
                P.do("tensor", mm_y, c_t, mark=f"ymm_{kq}")
                if q == 1:
                    P.mk(f"yfree_{hi_half}", c_t)
                P.w("vector", c_t, f"ymm_{kq}")
                if kq >= 1:
                    P.w("vector", c_sst, f"sst_{kq-1}", opt=True)
                P.do("vector", lambda e, dc=dc, qq=qq: e.scalar_tensor_tensor(
                    out=sst[qq % 2][:], in0=xnf2[dc % 2][:, qq * 512:(qq + 1) * 512],
                    scalar=dcolv[:, dc:dc + 1], in1=py[:], op0=OP.mult, op1=OP.add),
                    c_v, mark=f"sev_{kq}")
                P.w("sync", c_v, f"sev_{kq}")
                P.dma("sync", lambda e, dc=dc, qq=qq: e.dma_start(
                    out=bass.AP(s_hbm.tensor, dc * 262144 + qq * 512, [[2048, 128], [1, 512]]),
                    in_=sst[qq % 2][:]), c_sst, mark=f"sst_{kq}")

        def interleave(a, b):
            out = []
            na, nb = len(a), len(b)
            wa = na * 1.14
            ia = ib = 0
            for k in range(na + nb):
                if ia < na and ia * nb <= ib * wa:
                    out.append(a[ia]); ia += 1
                elif ib < nb:
                    out.append(b[ib]); ib += 1
                else:
                    out.append(a[ia]); ia += 1
            return out

        for dc in range(NDC):
            mix = []
            if dc == 0:
                mix += [lambda c=c: tok_Mdp(0, c) for c in range(4)]
                mix.append(lambda: tok_Msp(0))
            for half in range(2):
                for j in range(16):
                    mix.append(lambda dc=dc, half=half, j=j: tok_Mtile(dc, half, j))
                if half == 1 and dc + 1 < NDC:
                    mix += [lambda dc=dc, c=c: tok_Mdp(dc + 1, c) for c in range(4)]
                    mix.append(lambda dc=dc: tok_Msp(dc + 1))
                mix.append(lambda dc=dc, half=half: tok_My(dc, half))
            ffn = []
            for k in range(4):
                hi = dc * 4 + k
                ffn.append(lambda hi=hi: tok_Fw(hi))
                for c in range(4):
                    for hf in range(4):
                        ffn.append(lambda hi=hi, c=c, hf=hf: tok_Fa(hi, c, hf))
                    for hf in range(4):
                        ffn.append(lambda hi=hi, c=c, hf=hf: tok_Fb(hi, c, hf))
                    ffn.append(lambda hi=hi, c=c: tok_F(hi, c))
                    if c == 3:
                        ffn.append(lambda hi=hi: tok_Fpa(hi, 0))
                        ffn.append(lambda hi=hi: tok_Fpa(hi, 1))
                ffn.append(lambda hi=hi: tok_Fsync(hi))
            for tok in interleave(mix, ffn):
                tok()
        P.mk("p2_t_done", c_t)
        P.mk("p2_s_done", c_s)
        P.mk("p2_v_done", c_v)

    def build_p3(bufs):
        (vhat, vbst_t, wdb, gt, xt3, st3, ot3, ppb, pdn) = bufs

        def emit_pb_chunk(tb, rr):
            kk = tb * 32 + rr
            if rr >= 1:
                P.w("scalar", c_t, f"pbmm_{kk-1}")
            P.dma("scalar", lambda e, rr=rr, tb=tb: e.dma_start(
                out=vbst_t[kk % 2][:], in_=bass.AP(vb2.tensor, 2 * rr * 2048 + tb * 1024,
                                                   [[131072, 128], [2048, 2], [1, 1024]])),
                c_vbld, mark=f"vbld_{kk}")
            P.w("tensor", c_vbld, f"vbld_{kk}")
            if kk >= 1:
                P.w("tensor", c_s, f"pbev_{kk-1}")
            def mm_pb(e, kk=kk):
                for sub in range(2):
                    for half in range(2):
                        inst = e.matmul(ppb[sub][:, half * 512:(half + 1) * 512], s2b[:],
                                        vbst_t[kk % 2][:, sub * 1024 + half * 512:
                                                        sub * 1024 + (half + 1) * 512],
                                        start=True, stop=True)
                return inst
            P.do("tensor", mm_pb, c_t, mark=f"pbmm_{kk}")
            P.w("scalar", c_t, f"pbmm_{kk}")
            P.do("scalar", lambda e, rr=rr: e.copy(
                out=vhat[:, (2 * rr) * 1024:(2 * rr + 1) * 1024], in_=ppb[0][:]), c_s)
            P.do("scalar", lambda e, rr=rr: e.copy(
                out=vhat[:, (2 * rr + 1) * 1024:(2 * rr + 2) * 1024], in_=ppb[1][:]), c_s,
                mark=f"pbev_{kk}")

        def emit_dn_grp(tb, hc, grp, chase):
            k = tb * 16 + hc
            kg = k * 4 + grp
            if kg >= 2:
                P.w("gpsimd", c_t, f"dng_{kg-2}")
            P.dma("gpsimd", lambda e, hc=hc, grp=grp: e.dma_start(
                out=wdb[grp % 2][:], in_=bass.AP(wd_d.tensor, hc * 1048576 + grp * 2048,
                                                 [[8192, 128], [1, 2048]])),
                c_wd, mark=f"wdld_{kg}")
            if chase:
                P.w("tensor", c_s, f"pbev_{tb*32 + 8*grp + 7}")
            if grp == 0 and k >= 1:
                P.w("tensor", c_s, f"sil_{k-1}")
            P.w("tensor", c_wd, f"wdld_{kg}")
            def mm_dn(e, hc=hc, grp=grp):
                for rr in range(16):
                    r2 = grp * 16 + rr
                    for half in range(2):
                        inst = e.matmul(pdn[hc % 2][:, half * 512:(half + 1) * 512],
                                        wdb[grp % 2][:, rr * 128:(rr + 1) * 128],
                                        vhat[:, r2 * 1024 + half * 512: r2 * 1024 + (half + 1) * 512],
                                        start=(r2 == 0), stop=(r2 == 63))
                return inst
            P.do("tensor", mm_dn, c_t, mark=f"dng_{kg}")
            if grp == 3:
                P.mk(f"dnmm_{k}", c_t)

        def emit_dn_epi(tb, hc):
            k = tb * 16 + hc
            if k >= 2:
                P.w("sync", c_v, f"comb_{k-2}")
            P.dma("sync", lambda e, hc=hc, tb=tb: e.dma_start(
                out=xt3[hc % 2][:], in_=bass.AP(xT.tensor, hc * 262144 + tb * 1024, [[2048, 128], [1, 1024]])), c_xs)
            P.dma("sync", lambda e, hc=hc, tb=tb: e.dma_start(
                out=st3[hc % 2][:], in_=bass.AP(s_hbm.tensor, hc * 262144 + tb * 1024, [[2048, 128], [1, 1024]])),
                c_xs, mark=f"xsld_{k}")
            P.w("scalar", c_t, f"dnmm_{k}")
            if k >= 2:
                P.w("scalar", c_v, f"comb_{k-2}")
            P.do("scalar", lambda e, hc=hc, k=k: e.activation(
                out=gt[k % 2][:], in_=pdn[hc % 2][:], func=AF.Silu, scale=wsdp_eff[:, hc:hc + 1]),
                c_s, mark=f"sil_{k}")
            P.w("vector", c_s, f"sil_{k}")
            P.w("vector", c_xs, f"xsld_{k}")
            if k >= 2:
                P.w("vector", c_ost, f"ost_{k-2}")
            P.do("vector", lambda e, hc=hc, k=k: e.tensor_tensor(
                out=ot3[hc % 2][:], in0=st3[hc % 2][:], in1=gt[k % 2][:], op=OP.mult), c_v)
            P.do("vector", lambda e, hc=hc: e.tensor_tensor(
                out=ot3[hc % 2][:], in0=ot3[hc % 2][:], in1=xt3[hc % 2][:], op=OP.add),
                c_v, mark=f"comb_{k}")
            P.w("sync", c_v, f"comb_{k}")
            P.dma("sync", lambda e, hc=hc, tb=tb: e.dma_start(
                out=bass.AP(out_d.tensor, hc * 262144 + tb * 1024, [[2048, 128], [1, 1024]]),
                in_=ot3[hc % 2][:]), c_ost, mark=f"ost_{tb*16+hc}")

        for tb in range(2):
            if tb == 0:
                P.w("sync", c_vbst, f"vbst_{NI-1}")
                P.w("scalar", c_vbst, f"vbst_{NI-1}")
                P.w("sync", c_sst, "sst_63")
                P.w("tensor", c_v, "p2_v_done")
                P.w("tensor", c_s, "p2_s_done")
            else:
                P.w("sync", c_t, f"dnmm_{(tb-1)*16+15}")
            for grp in range(4):
                for rr in range(8 * grp, 8 * grp + 8):
                    emit_pb_chunk(tb, rr)
                emit_dn_grp(tb, 0, grp, chase=True)
            emit_dn_epi(tb, 0)
            for hc in range(1, 16):
                for grp in range(4):
                    emit_dn_grp(tb, hc, grp, chase=False)
                emit_dn_epi(tb, hc)
        P.w("sync", c_ost, val=16 * 32)

    p0()
    p1a()
    ps1.close()
    ps1b = contextlib.ExitStack()
    prep1 = ps1b.enter_context(nc.psum_tensor("prep1a", [128, 512], F32))
    pw1 = ps1b.enter_context(nc.psum_tensor("pw1", [96, 1], F32))
    p1a_epi(prep1, pw1)
    ps1b.close()
    ps1c = contextlib.ExitStack()
    ps_xd = ps1c.enter_context(nc.psum_tensor("ps_xd", [96, 2048], F32))
    ppa = [ps1c.enter_context(nc.psum_tensor(f"ppa{i}", [128, 512], F32)) for i in range(2)]
    p1b(ps_xd, ppa)
    p1b_epi(ps_xd, ppa)
    ps1c.close()
    ps1d = contextlib.ExitStack()
    pxh = [ps1d.enter_context(nc.psum_tensor(f"pxh{i}", [128, 512], F32)) for i in range(2)]
    p1c(pxh)
    ps1d.close()

    p1sc.close()
    for eng in ("sync", "gpsimd"):
        P.w(eng, c_t, "p1c_t_done")
        P.w(eng, c_s, "p1c_s_done")
        P.w(eng, c_v, "xdbl_v")
        P.w(eng, c_xnst, val=16 * NDC)
        P.w(eng, c_xbst, val=16 * 8 * NDC)

    p2sc = contextlib.ExitStack()
    sb2 = lambda n, s, dt: p2sc.enter_context(nc.sbuf_tensor(n, s, dt))
    wub = [sb2(f"wub{i}", [128, 2048], BF16) for i in range(2)]
    wgb = [sb2(f"wgb{i}", [128, 2048], BF16) for i in range(2)]
    ub = sb2("ub", [128, 2048], BF16)
    sgb = sb2("sgb", [128, 2048], BF16)
    vtile = sb2("vtile", [128, 2048], BF16)
    vA = vtile
    xnf2 = [sb2(f"xnf2_{i}", [128, 2048], BF16) for i in range(2)]
    nlsp = [sb2(f"nlsp{i}", [128, 2048], BF16) for i in range(2)]
    _nx = sb2("ndx0", [128, 2048], BF16)
    ndx = [_nx, _nx]
    dAh = [sb2(f"dAh{i}", [128, 1024], F32) for i in range(3)]
    _bx = sb2("BXh0", [128, 1024], BF16)
    BXh = [_bx, _bx, _bx]
    hCs = [sb2(f"hCs{i}", [128, 1024], BF16) for i in range(16)]
    stcol = sb2("stcol", [128, 16], F32)
    sigt = sb2("sigt", [128, 512], F32)
    _ss = sb2("sst0", [128, 512], F32)
    sst = [_ss, _ss]
    ps2 = contextlib.ExitStack()
    pu = [ps2.enter_context(nc.psum_tensor(f"pu{i}", [128, 512], F32)) for i in range(2)]
    pg0 = ps2.enter_context(nc.psum_tensor("pg0", [128, 512], F32))
    pg = [pg0, pg0]
    ppa2 = ps2.enter_context(nc.psum_tensor("ppa2", [128, 512], F32))
    prep = [ps2.enter_context(nc.psum_tensor(f"prep{i}", [128, 512], F32)) for i in range(2)]
    prep3 = ps2.enter_context(nc.psum_tensor("prep3", [128, 512], F32))
    py = ps2.enter_context(nc.psum_tensor("py", [128, 512], F32))
    build_p2((wub, wgb, ub, sgb, vtile, vA, xnf2, nlsp, ndx, dAh, BXh,
              hCs, stcol, sigt, sst, pu, pg, ppa2, prep, prep3, py))
    ps2.close()

    p2sc.close()
    xhsc.close()
    p12.close()
    for eng in ("sync", "gpsimd"):
        P.w(eng, c_t, "p2_t_done")
        P.w(eng, c_s, "p2_s_done")
        P.w(eng, c_v, "p2_v_done")
        P.w(eng, c_sst, val=16 * 64)
        P.w(eng, c_vbst, val=16 * 2 * NI)

    p3sc = contextlib.ExitStack()
    sb3 = lambda n, s, dt: p3sc.enter_context(nc.sbuf_tensor(n, s, dt))
    vhat = sb3("vhat", [128, 64 * 1024], BF16)
    vbst_t = [sb3(f"vbst{i}", [128, 2048], BF16) for i in range(2)]
    wdb = [sb3(f"wdb{i}", [128, 2048], BF16) for i in range(2)]
    gt = [sb3(f"gt{i}", [128, 1024], F32) for i in range(2)]
    xt3 = [sb3(f"xt3_{i}", [128, 1024], F32) for i in range(2)]
    st3 = [sb3(f"st3_{i}", [128, 1024], F32) for i in range(2)]
    ot3 = [sb3(f"ot3_{i}", [128, 1024], F32) for i in range(2)]
    ps3 = contextlib.ExitStack()
    ppb = [ps3.enter_context(nc.psum_tensor(f"ppb{i}", [128, 1024], F32)) for i in range(2)]
    pdn = [ps3.enter_context(nc.psum_tensor(f"pdn{i}", [128, 1024], F32)) for i in range(2)]
    build_p3((vhat, vbst_t, wdb, gt, xt3, st3, ot3, ppb, pdn))
    ps3.close()
    p3sc.close()

    with nc.Block() as blk:
        blk.sync(lambda e: [op(e) for op in P.ops["sync"]])
        blk.gpsimd(lambda e: [op(e) for op in P.ops["gpsimd"]])
        blk.tensor(lambda e: [op(e) for op in P.ops["tensor"]])
        blk.scalar(lambda e: [op(e) for op in P.ops["scalar"]])
        blk.vector(lambda e: [op(e) for op in P.ops["vector"]])
    es.close()
    build_program._last_P = P
    return nc


def kernel(**inputs):
    lay = host_layouts(inputs)
    cst = host_consts()
    x = np.asarray(inputs["L0_attn_x"], np.float32)
    nc = build_program()
    base = {}
    base.update(lay)
    base.update(cst)
    in_maps = []
    for b in range(8):
        m = dict(base)
        m["xT"] = np.ascontiguousarray(x[b].T)
        in_maps.append(m)
    res = run_bass_kernel_spmd(nc, in_maps, core_ids=list(range(8)))
    kernel._last_res = res
    out = np.empty((8, L, D), np.float32)
    for b in range(8):
        out[b] = res.results[b]["out"].T
    return out


# version 35 (final)
# speedup vs baseline: 1.0803x; 1.0230x over previous
import numpy as np
import ml_dtypes

import concourse.bass as bass
import concourse.mybir as mybir
from concourse.alu_op_type import AluOpType as OP
from concourse.bass_utils import run_bass_kernel_spmd

F32 = mybir.dt.float32
BF16 = mybir.dt.bfloat16
AF = mybir.ActivationFunctionType
BF = ml_dtypes.bfloat16

D = 2048
L = 2048
NDC = 16
NI = 64
ENGS = ("sync", "gpsimd", "tensor", "scalar", "vector")

DEBUG_TAPS = False


class Ctr:
    def __init__(self, sem):
        self.sem = sem
        self.n = 0


class Prog:

    def __init__(self):
        self.ops = {k: [] for k in ENGS}
        self.marks = {}
        self.trace = {k: [] for k in ENGS}

    def w(self, eng, ctr, key=None, val=None, opt=False):
        sem = ctr.sem
        self.trace[eng].append(("wait", id(ctr), key, val, opt))
        if key is not None:
            if opt:
                self.ops[eng].append(
                    lambda e: e.wait_ge(sem, self.marks[key]) if key in self.marks else None)
            else:
                self.ops[eng].append(lambda e: e.wait_ge(sem, self.marks[key]))
        else:
            self.ops[eng].append(lambda e: e.wait_ge(sem, val))

    def do(self, eng, fn, ctr=None, inc=1, mark=None):
        if ctr is not None:
            ctr.n += inc
            if mark is not None:
                self.marks[mark] = ctr.n
            self.trace[eng].append(("inc", id(ctr), inc, mark))
            self.ops[eng].append(lambda e: fn(e).then_inc(ctr.sem, inc))
        else:
            self.trace[eng].append(("op", None, 0, None))
            self.ops[eng].append(fn)

    def dma(self, eng, fn, ctr, mark=None):
        self.do(eng, fn, ctr, 16, mark)

    def mk(self, key, ctr):
        self.marks[key] = ctr.n


def _hadamard(n):
    i = np.arange(n)
    a = i[:, None] & i[None, :]
    par = np.zeros((n, n), np.int64)
    while a.any():
        par ^= a & 1
        a >>= 1
    return (1.0 - 2.0 * par).astype(np.float32)


def host_consts():
    c = {}
    H16 = _hadamard(16)
    H64 = _hadamard(64)
    c["h128f"] = _hadamard(128).astype(np.float32)
    c["h128b"] = _hadamard(128).astype(BF)
    p = np.arange(128)
    c["s1b"] = ((p[:, None] >> 4 == p[None, :] >> 4) * H16[np.ix_(p & 15, p & 15)]).astype(BF)
    c["s2b"] = ((p[:, None] >> 6 == p[None, :] >> 6) * H64[np.ix_(p & 63, p & 63)]).astype(BF)
    Rp = np.zeros((16, 128, 128), np.float32)
    Sj = np.zeros((16, 128, 128), np.float32)
    for j in range(16):
        Rp[j, 8 * j + (p >> 4), p] = 1.0
        Sj[j, p, 8 * j + (p >> 4)] = 1.0
    c["rpb"] = np.ascontiguousarray(Rp.transpose(1, 0, 2).reshape(128, 2048)).astype(BF)
    c["sjb"] = np.ascontiguousarray(Sj.transpose(1, 0, 2).reshape(128, 2048)).astype(BF)
    REPB = np.zeros((32, 128), np.float32)
    REPC = np.zeros((32, 128), np.float32)
    REPB[p & 15, p] = 1.0
    REPC[16 + (p & 15), p] = 1.0
    c["repbb"] = REPB.astype(BF)
    c["repcb"] = REPC.astype(BF)
    c["ones128b"] = np.ones((128, 1), BF)
    c["onesk1f"] = np.ones((1, 128), np.float32)
    return c


def host_layouts(inp):
    d = {}
    Wxp = np.asarray(inp["L0_attn_W_xproj"], np.float32)
    Wxp_r = np.concatenate([Wxp[64:80], Wxp[80:96], Wxp[0:64]], axis=0)
    d["wxp"] = np.ascontiguousarray(
        Wxp_r.reshape(96, 16, 128).transpose(2, 1, 0).reshape(128, 16 * 96))
    d["wdt"] = np.ascontiguousarray(np.asarray(inp["L0_attn_W_dt"], np.float32).T)
    d["bdt"] = np.ascontiguousarray(np.asarray(inp["L0_attn_b_dt"], np.float32).reshape(16, 128).T)
    d["dvec"] = np.ascontiguousarray(np.asarray(inp["L0_attn_D"], np.float32).reshape(16, 128).T)
    Alog = np.asarray(inp["L0_attn_A_log"], np.float32)
    pidx = np.arange(128)
    A_dev = np.empty((128, 256), np.float32)
    for dc in range(16):
        for j in range(16):
            A_dev[:, dc * 16 + j] = Alog[dc * 128 + 8 * j + (pidx >> 4), pidx & 15]
    d["adev"] = A_dev
    d["sug"] = np.ascontiguousarray(np.asarray(inp["L0_ffn_SU_ug"], np.float32).reshape(16, 128).T)
    ws = np.asarray(inp["L0_ffn_Wscale_ug"], np.float32)
    d["wsug"] = np.ascontiguousarray(
        np.concatenate([ws[:8192].reshape(64, 128).T, ws[8192:].reshape(64, 128).T], axis=1))
    d["sudp"] = np.ascontiguousarray(np.asarray(inp["L0_ffn_SU_dp"], np.float32).reshape(64, 128).T)
    d["wsdp"] = np.ascontiguousarray(np.asarray(inp["L0_ffn_Wscale_dp"], np.float32).reshape(16, 128).T)
    q = np.arange(128)
    r1 = np.arange(16)
    f1 = ((q[:, None] & 15) << 7) | ((q[:, None] >> 4) << 4) | r1[None, :]
    for name, key in (("L0_ffn_W_up", "wu"), ("L0_ffn_W_gate", "wg")):
        W = np.asarray(inp[name], np.float32).reshape(64, 128, 2048)
        d[key] = np.ascontiguousarray(W[:, :, f1].transpose(0, 2, 3, 1).reshape(64, 128, 2048))
    q2 = np.arange(128)
    r2 = np.arange(64)
    i2 = ((q2[:, None] & 63) << 7) | ((q2[:, None] >> 6) << 6) | r2[None, :]
    Wd = np.asarray(inp["L0_ffn_W_down"], np.float32).reshape(16, 128, 8192)
    d["wd"] = np.ascontiguousarray(Wd[:, :, i2].transpose(0, 2, 3, 1).reshape(16, 128, 8192))
    return d


def build_program():
    nc = bass.Bass("TRN2", target_bir_lowering=False)
    P = Prog()

    dram_in = lambda n, s, dt=F32: nc.dram_tensor(n, s, dt, kind="ExternalInput").ap()
    xT = dram_in("xT", [D, L])
    wxp = dram_in("wxp", [128, 16 * 96])
    wdt_d = dram_in("wdt", [64, 2048])
    bdt_d = dram_in("bdt", [128, 16])
    dvec_d = dram_in("dvec", [128, 16])
    adev_d = dram_in("adev", [128, 256])
    sug_d = dram_in("sug", [128, 16])
    wsug_d = dram_in("wsug", [128, 128])
    sudp_d = dram_in("sudp", [128, 64])
    wsdp_d = dram_in("wsdp", [128, 16])
    wu_d = dram_in("wu", [64, 128, 2048])
    wg_d = dram_in("wg", [64, 128, 2048])
    wd_d = dram_in("wd", [16, 128, 8192])
    h128f_d = dram_in("h128f", [128, 128])
    h128b_d = dram_in("h128b", [128, 128], BF16)
    s1b_d = dram_in("s1b", [128, 128], BF16)
    s2b_d = dram_in("s2b", [128, 128], BF16)
    rpb_d = dram_in("rpb", [128, 2048], BF16)
    sjb_d = dram_in("sjb", [128, 2048], BF16)
    repbb_d = dram_in("repbb", [32, 128], BF16)
    repcb_d = dram_in("repcb", [32, 128], BF16)
    ones128b_d = dram_in("ones128b", [128, 1], BF16)
    onesk1f_d = dram_in("onesk1f", [1, 128])

    _ik = "ExternalOutput" if DEBUG_TAPS else "Internal"
    if DEBUG_TAPS:
        dbg_brep = nc.dram_tensor("dbg_brep", [128, 2048], BF16, kind="ExternalOutput").ap()
        dbg_crep = nc.dram_tensor("dbg_crep", [128, 2048], BF16, kind="ExternalOutput").ap()
        dbg_dpre = nc.dram_tensor("dbg_dpre", [64, 2048], BF16, kind="ExternalOutput").ap()
        dbg_nlsp = nc.dram_tensor("dbg_nlsp", [128, 2048], BF16, kind="ExternalOutput").ap()
        dbg_ndx = nc.dram_tensor("dbg_ndx", [128, 2048], BF16, kind="ExternalOutput").ap()
        dbg_dA = nc.dram_tensor("dbg_dA", [128, 1024], F32, kind="ExternalOutput").ap()
        dbg_BX = nc.dram_tensor("dbg_BX", [128, 1024], BF16, kind="ExternalOutput").ap()
        dbg_h = nc.dram_tensor("dbg_h", [128, 1024], BF16, kind="ExternalOutput").ap()
    xn_hbm = nc.dram_tensor("xn_hbm", [D, L], F32, kind=_ik).ap()
    s_hbm = nc.dram_tensor("s_hbm", [D, L], F32, kind=_ik).ap()
    xb2 = nc.dram_tensor("xb2", [128, 16 * 2048], BF16, kind=_ik).ap()
    vb2 = nc.dram_tensor("vb2", [128, 64 * 2048], BF16, kind=_ik).ap()
    out_d = nc.dram_tensor("out", [D, L], F32, kind="ExternalOutput").ap()

    import contextlib
    es = contextlib.ExitStack()
    sb = lambda n, s, dt: es.enter_context(nc.sbuf_tensor(n, s, dt))
    mksem = lambda n: Ctr(es.enter_context(nc.semaphore(n)))

    s2b = es.enter_context(nc.sbuf_tensor("s2b_s", [128, 128], BF16))
    wsdp_eff = es.enter_context(nc.sbuf_tensor("wsdp_eff", [128, 16], F32))
    import contextlib as _ctl
    p12 = _ctl.ExitStack()
    sb = lambda n, s, dt: p12.enter_context(nc.sbuf_tensor(n, s, dt))
    wxpb = sb("wxpb", [128, 1536], BF16)
    wdtb = sb("wdtb", [64, 2048], BF16)
    bdtn = sb("bdtn", [128, 16], F32)
    dcolv = sb("dcolv", [128, 16], F32)
    posA = sb("posA", [128, 256], F32)
    h128su = sb("h128su", [128, 2048], BF16)
    h128b = sb("h128b_s", [128, 128], BF16)
    s1b = sb("s1b_s", [128, 128], BF16)
    rpb = sb("rpb_s", [128, 2048], BF16)
    sjb = sb("sjb_s", [128, 2048], BF16)
    repbb = sb("repbb_s", [32, 128], BF16)
    repcb = sb("repcb_s", [32, 128], BF16)
    ones128b = sb("ones128b_s", [128, 1], BF16)
    onesk1f = sb("onesk1f_s", [1, 128], F32)
    wsu_eff = sb("wsu_eff", [128, 64], F32)
    wsg_eff = sb("wsg_eff", [128, 64], F32)
    su1c = sb("su1c", [128, 16], F32)
    w1col = sb("w1col", [96, 1], F32)
    BC32 = sb("BC32", [32, 2048], BF16)
    dpreb = sb("dpreb", [64, 2048], BF16)
    B_rep = sb("B_rep", [128, 2048], BF16)
    C_rep = sb("C_rep", [128, 2048], BF16)

    c_ld = mksem("c_ld")
    c_x = mksem("c_x")
    c_t = mksem("c_t")
    c_s = mksem("c_s")
    c_v = mksem("c_v")
    c_g = mksem("c_g")
    c_xnst = mksem("c_xnst")
    c_xbst = mksem("c_xbst")
    c_xbld = mksem("c_xbld")
    c_w = mksem("c_w")
    c_xnld = mksem("c_xnld")
    c_vbst = mksem("c_vbst")
    c_sst = mksem("c_sst")
    c_vbld = mksem("c_vbld")
    c_wd = mksem("c_wd")
    c_xs = mksem("c_xs")
    c_ost = mksem("c_ost")
    c_ph = mksem("c_ph")
    c_dbg = mksem("c_dbg")

    M = P.marks

    def p0():
        loads = [
            (bdtn, bdt_d), (dcolv, dvec_d), (posA, adev_d), (su1c, sug_d),
            (h128b, h128b_d), (s1b, s1b_d), (s2b, s2b_d), (rpb, rpb_d),
            (sjb, sjb_d), (repbb, repbb_d), (repcb, repcb_d),
            (ones128b, ones128b_d), (onesk1f, onesk1f_d), (sudp_s, sudp_d),
            (h128f_s, h128f_d), (wdtb if False else None, None),
        ]
        for dst, src in loads:
            if dst is None:
                continue
            P.dma("sync", lambda e, d=dst, s=src: e.dma_start(out=d[:], in_=s), c_ld)
        P.dma("sync", lambda e: e.dma_start(out=wsu_eff[:], in_=bass.AP(wsug_d.tensor, 0, [[128, 128], [1, 64]])), c_ld)
        P.dma("sync", lambda e: e.dma_start(out=wsg_eff[:], in_=bass.AP(wsug_d.tensor, 64, [[128, 128], [1, 64]])), c_ld)
        P.dma("sync", lambda e: e.dma_start(out=wsdp_eff[:], in_=wsdp_d), c_ld)
        P.dma("sync", lambda e: e.dma_start(out=murep[:, 0:1536], in_=wxp), c_ld)
        P.dma("sync", lambda e: e.dma_start(out=xdblt[0:64, :], in_=wdt_d), c_ld)
        P.mk("p0_ld", c_ld)

        P.w("scalar", c_ld, "p0_ld")
        P.do("scalar", lambda e: e.activation(out=posA[:], in_=posA[:], func=AF.Exp), c_s)
        P.do("scalar", lambda e: e.activation(out=posA[:], in_=posA[:], func=AF.Copy, scale=-1.0), c_s)
        P.do("scalar", lambda e: e.activation(out=su1c[:], in_=su1c[:], func=AF.Copy,
                                              scale=float(1.0 / (64.0 * np.sqrt(2048.0)))), c_s)
        P.do("scalar", lambda e: e.activation(out=wsg_eff[:], in_=wsg_eff[:], func=AF.Copy, scale=64.0), c_s)
        P.do("scalar", lambda e: e.activation(out=wsdp_eff[:], in_=wsdp_eff[:], func=AF.Copy, scale=64.0), c_s)
        P.do("scalar", lambda e: e.copy(out=wxpb[:], in_=murep[:, 0:1536]), c_s)
        P.do("scalar", lambda e: e.copy(out=wdtb[:], in_=xdblt[0:64, :]), c_s, mark="p0_s")

        P.w("vector", c_s, "p0_s")
        P.do("vector", lambda e: e.tensor_tensor(out=wsu_eff[:], in0=wsu_eff[:], in1=sudp_s[:], op=OP.mult), c_v)
        P.do("vector", lambda e: e.tensor_scalar_mul(out=wsu_eff[:], in0=wsu_eff[:],
                                                     scalar1=float(1.0 / np.sqrt(8192.0))), c_v)
        for fc in range(16):
            P.do("vector", lambda e, fc=fc: e.tensor_scalar_mul(
                out=h128su[:, fc * 128:(fc + 1) * 128], in0=h128f_s[:],
                scalar1=su1c[:, fc:fc + 1]), c_v, mark="p0_v")

    xhsc = contextlib.ExitStack()
    xhat = xhsc.enter_context(nc.sbuf_tensor("xhat", [128, 16 * 2048], BF16))
    p1sc = contextlib.ExitStack()
    sb1 = lambda n, s, dt: p1sc.enter_context(nc.sbuf_tensor(n, s, dt))
    sudp_s = p1sc.enter_context(nc.sbuf_tensor("sudp_s", [128, 64], F32))
    h128f_s = p1sc.enter_context(nc.sbuf_tensor("h128f_s", [128, 128], F32))
    murep = sb1("murep", [128, 2048], F32)
    rrep = sb1("rrep", [128, 2048], F32)
    xdblt = sb1("xdblt", [96, 2048], F32)
    tmp96 = sb1("tmp96", [96, 2048], F32)
    xt = [sb1(f"xt{i}", [128, 2048], F32) for i in range(2)]
    xbt = [sb1(f"xbt{i}", [128, 2048], BF16) for i in range(2)]
    xsq = [sb1(f"xsq{i}", [128, 2048], BF16) for i in range(2)]
    xnf = [sb1(f"xnf{i}", [128, 2048], F32) for i in range(2)]
    xpa = sb1("xpa", [128, 2048], BF16)
    xbst = [sb1(f"xbst{i}", [128, 4096], BF16) for i in range(2)]

    ps1 = contextlib.ExitStack()
    ps_sum = ps1.enter_context(nc.psum_tensor("ps_sum", [1, 2048], F32))
    ps_sq = ps1.enter_context(nc.psum_tensor("ps_sq", [1, 2048], F32))

    def p1a():
        for dc in range(NDC):
            b = dc % 2
            if dc >= 2:
                P.w("sync", c_s, f"p1a_s_{dc-2}")
                P.w("sync", c_v, f"p1a_v_{dc-2}")
            P.dma("sync", lambda e, b=b, dc=dc: e.dma_start(out=xt[b][:], in_=xT[dc * 128:(dc + 1) * 128, :]),
                  c_x, mark=f"p1a_x_{dc}")
            P.w("scalar", c_x, f"p1a_x_{dc}")
            if dc >= 2:
                P.w("scalar", c_t, f"p1a_t_{dc-2}")
            P.do("scalar", lambda e, b=b: e.copy(out=xbt[b][:], in_=xt[b][:]), c_s, mark=f"p1a_s_{dc}")
            P.w("vector", c_x, f"p1a_x_{dc}")
            if dc >= 2:
                P.w("vector", c_t, f"p1a_t_{dc-2}")
            P.do("vector", lambda e, b=b: e.tensor_tensor(out=xsq[b][:], in0=xt[b][:], in1=xt[b][:], op=OP.mult),
                 c_v, mark=f"p1a_v_{dc}")
            P.w("tensor", c_s, f"p1a_s_{dc}")
            P.w("tensor", c_v, f"p1a_v_{dc}")
            for c in range(4):
                sl = slice(c * 512, (c + 1) * 512)
                P.do("tensor", lambda e, b=b, dc=dc, sl=sl: e.matmul(
                    ps_sum[0:1, sl], ones128b[:], xbt[b][:, sl],
                    start=(dc == 0), stop=(dc == NDC - 1)), c_t)
                P.do("tensor", lambda e, b=b, dc=dc, sl=sl: e.matmul(
                    ps_sq[0:1, sl], ones128b[:], xsq[b][:, sl],
                    start=(dc == 0), stop=(dc == NDC - 1)), c_t,
                    mark=f"p1a_t_{dc}" if c == 3 else None)

    def p1a_epi(prep1, pw1):
        P.w("vector", c_t, "p1a_t_15")
        P.do("vector", lambda e: e.tensor_scalar_mul(out=tmp96[0:1, :], in0=ps_sum[0:1, :], scalar1=float(1.0 / D)), c_v)
        P.do("vector", lambda e: e.tensor_scalar_mul(out=xdblt[0:1, :], in0=ps_sq[0:1, :], scalar1=float(1.0 / D)),
             c_v, mark="rows_v")
        P.w("scalar", c_v, "rows_v")
        P.do("scalar", lambda e: e.activation(out=murep[0:1, :], in_=tmp96[0:1, :], func=AF.Square), c_s, mark="musq_s")
        P.w("vector", c_s, "musq_s")
        P.do("vector", lambda e: e.tensor_tensor(out=xdblt[0:1, :], in0=xdblt[0:1, :], in1=murep[0:1, :], op=OP.subtract), c_v)
        P.do("vector", lambda e: e.tensor_scalar_add(out=xdblt[0:1, :], in0=xdblt[0:1, :], scalar1=1e-5),
             c_v, mark="var_v")
        P.w("scalar", c_v, "var_v")
        P.do("scalar", lambda e: e.activation(out=xdblt[0:1, :], in_=xdblt[0:1, :], func=AF.Sqrt),
             c_s, mark="sqrt_s")
        P.w("vector", c_s, "sqrt_s")
        P.do("vector", lambda e: e.reciprocal(out=xdblt[0:1, :], in_=xdblt[0:1, :]), c_v, mark="rrow_v")
        P.w("tensor", c_v, "rrow_v")
        k = 0
        for dst, rowsrc in ((murep, tmp96), (rrep, xdblt)):
            for c in range(4):
                sl = slice(c * 512, (c + 1) * 512)
                if k >= 1:
                    P.w("tensor", c_s, f"repev_{k-1}")
                P.do("tensor", lambda e, rowsrc=rowsrc, sl=sl: e.matmul(
                    prep1[:], onesk1f[:], rowsrc[0:1, sl], start=True, stop=True),
                    c_t, mark=f"repmm_{k}")
                P.w("scalar", c_t, f"repmm_{k}")
                P.do("scalar", lambda e, dst=dst, sl=sl: e.copy(out=dst[:, sl], in_=prep1[:]),
                     c_s, mark=f"repev_{k}")
                k += 1


    def p1b(ps_xd, ppa):
        for dc in range(NDC):
            b = dc % 2
            if dc < 2:
                P.w("sync", c_s, "p1a_s_15")
                P.w("sync", c_v, "p1a_v_15")
            if dc >= 2:
                P.w("sync", c_v, f"p1b_xn_{dc-2}")
            P.dma("sync", lambda e, b=b, dc=dc: e.dma_start(out=xt[b][:], in_=xT[dc * 128:(dc + 1) * 128, :]),
                  c_x, mark=f"p1b_x_{dc}")
            P.w("vector", c_x, f"p1b_x_{dc}")
            if dc == 0:
                P.w("vector", c_s, "repev_7")
            if dc >= 2:
                P.w("vector", c_xnst, f"p1b_xnst_{dc-2}")
                P.w("vector", c_s, f"p1b_cast_{dc-2}")
            P.do("vector", lambda e, b=b: e.tensor_tensor(out=xnf[b][:], in0=xt[b][:], in1=murep[:], op=OP.subtract), c_v)
            P.do("vector", lambda e, b=b: e.tensor_tensor(out=xnf[b][:], in0=xnf[b][:], in1=rrep[:], op=OP.mult),
                 c_v, mark=f"p1b_xn_{dc}")
            P.w("gpsimd", c_v, f"p1b_xn_{dc}")
            P.dma("gpsimd", lambda e, b=b, dc=dc: e.dma_start(out=xn_hbm[dc * 128:(dc + 1) * 128, :], in_=xnf[b][:]),
                  c_xnst, mark=f"p1b_xnst_{dc}")
            P.w("scalar", c_v, f"p1b_xn_{dc}")
            if dc >= 2:
                P.w("scalar", c_t, f"p1b_t_{dc-2}")
            P.do("scalar", lambda e, b=b: e.copy(out=xbt[b][:], in_=xnf[b][:]), c_s, mark=f"p1b_cast_{dc}")
            P.w("tensor", c_s, f"p1b_cast_{dc}")
            for c in range(4):
                sl = slice(c * 512, (c + 1) * 512)
                P.do("tensor", lambda e, b=b, dc=dc, sl=sl: e.matmul(
                    ps_xd[0:96, sl], wxpb[:, dc * 96:(dc + 1) * 96], xbt[b][:, sl],
                    start=(dc == 0), stop=(dc == NDC - 1)), c_t)
            for c in range(4):
                sl = slice(c * 512, (c + 1) * 512)
                k = dc * 4 + c
                if k >= 2:
                    P.w("tensor", c_s, f"p1b_pa_{k-2}")
                P.do("tensor", lambda e, b=b, dc=dc, sl=sl, c=c: e.matmul(
                    ppa[c % 2][:], h128su[:, dc * 128:(dc + 1) * 128], xbt[b][:, sl],
                    start=True, stop=True), c_t, mark=f"p1b_pamm_{k}" + ("" if c < 3 else "_last"))
                if c == 3:
                    P.mk(f"p1b_t_{dc}", c_t)
                P.w("scalar", c_t, f"p1b_pamm_{k}" + ("" if c < 3 else "_last"))
                if dc >= 1 and c == 0:
                    P.w("scalar", c_xbst, f"p1b_xbst_{dc-1}")
                P.do("scalar", lambda e, sl=sl, c=c: e.copy(out=xpa[:, sl], in_=ppa[c % 2][:]),
                     c_s, mark=f"p1b_pa_{k}")
            P.w("gpsimd", c_s, f"p1b_pa_{dc*4+3}")
            for g in range(8):
                P.dma("gpsimd", lambda e, g=g, dc=dc: e.dma_start(
                    out=bass.AP(xb2.tensor, (g * 16 + dc) * 16 * 2048, [[2048, 16], [1, 2048]]),
                    in_=xpa[g * 16:(g + 1) * 16, :]),
                    c_xbst, mark=f"p1b_xbst_{dc}" if g == 7 else None)

    def p1b_epi(ps_xd, ppa):
        P.w("vector", c_t, "p1b_t_15")
        P.do("vector", lambda e: e.tensor_copy(xdblt[:], ps_xd[0:96, :]), c_v, mark="xdbl_v")
        P.w("scalar", c_v, "xdbl_v")
        P.do("scalar", lambda e: e.copy(out=BC32[:], in_=xdblt[0:32, :]), c_s)
        P.do("scalar", lambda e: e.copy(out=dpreb[0:32, :], in_=xdblt[32:64, :]), c_s)
        P.do("scalar", lambda e: e.copy(out=dpreb[32:64, :], in_=xdblt[64:96, :]), c_s, mark="bc_s")
        P.w("tensor", c_s, "bc_s")
        k = 0
        for st, dst in ((repbb, B_rep), (repcb, C_rep)):
            for c in range(4):
                sl = slice(c * 512, (c + 1) * 512)
                if k >= 2:
                    P.w("tensor", c_s, f"bcrep_s_{k-2}")
                P.do("tensor", lambda e, st=st, sl=sl, k=k: e.matmul(
                    ppa[k % 2][:], st[:], BC32[:, sl], start=True, stop=True), c_t, mark=f"bcrep_t_{k}")
                P.w("scalar", c_t, f"bcrep_t_{k}")
                P.do("scalar", lambda e, dst=dst, sl=sl, k=k: e.copy(out=dst[:, sl], in_=ppa[k % 2][:]),
                     c_s, mark=f"bcrep_s_{k}")
                k += 1

    def p1c(pxh):
        if DEBUG_TAPS:
            P.w("sync", c_s, "bcrep_s_7")
            P.dma("sync", lambda e: e.dma_start(out=dbg_brep[:], in_=B_rep[:]), c_dbg)
            P.dma("sync", lambda e: e.dma_start(out=dbg_crep[:], in_=C_rep[:]), c_dbg)
            P.dma("sync", lambda e: e.dma_start(out=dbg_dpre[:], in_=dpreb[:]), c_dbg)
        P.w("sync", c_xbst, f"p1b_xbst_{NDC-1}")
        P.w("tensor", c_s, "bcrep_s_7")
        P.w("tensor", c_v, "xdbl_v")
        for k in range(8):
            if k >= 2:
                P.w("sync", c_t, f"p1c_t_{k-2}")
            P.dma("sync", lambda e, k=k: e.dma_start(
                out=xbst[k % 2][:], in_=bass.AP(xb2.tensor, k * 4096, [[32768, 128], [1, 4096]])),
                c_xbld, mark=f"p1c_ld_{k}")
            P.w("tensor", c_xbld, f"p1c_ld_{k}")
            for c in range(8):
                nmm = 8 * k + c
                if nmm >= 2:
                    P.w("tensor", c_s, f"p1c_s_{nmm-2}")
                P.do("tensor", lambda e, k=k, c=c, nmm=nmm: e.matmul(
                    pxh[nmm % 2][:], s1b[:], xbst[k % 2][:, c * 512:(c + 1) * 512],
                    start=True, stop=True), c_t, mark=f"p1c_mm_{nmm}")
                if c == 7:
                    P.mk(f"p1c_t_{k}", c_t)
                P.w("scalar", c_t, f"p1c_mm_{nmm}")
                P.do("scalar", lambda e, nmm=nmm: e.copy(
                    out=xhat[:, nmm * 512:(nmm + 1) * 512], in_=pxh[nmm % 2][:]),
                    c_s, mark=f"p1c_s_{nmm}")
        P.mk("p1c_s_done", c_s)
        P.mk("p1c_t_done", c_t)
        P.mk("p1c_v_done", c_v)

    def build_p2(bufs):
        (wub, wgb, ub, sgb, vtile, vA, xnf2, nlsp, ndx, dAh, BXh, hCs,
         stcol, sigt, sst, pu, pg, ppa2, prep, prep3, py) = bufs
        prepd = [prep[0], ppa2]
        ppa3 = prep3
        prepx = [prep[1], py]

        def tok_Fw(hi):
            if hi >= 2:
                P.w("gpsimd", c_t, f"fdone_{hi-2}")
            P.dma("gpsimd", lambda e, hi=hi: e.dma_start(
                out=wub[hi % 2][:], in_=bass.AP(wu_d.tensor, hi * 262144, [[2048, 128], [1, 2048]])), c_w)
            P.dma("gpsimd", lambda e, hi=hi: e.dma_start(
                out=wgb[hi % 2][:], in_=bass.AP(wg_d.tensor, hi * 262144, [[2048, 128], [1, 2048]])),
                c_w, mark=f"w_ld_{hi}")

        def tok_Fa(hi, c, hf):
            wpar = hi % 2
            k = hi * 4 + c
            if hf == 0:
                if c == 0:
                    P.w("tensor", c_w, f"w_ld_{hi}")
                    if hi == 0:
                        P.w("tensor", c_s, "p1c_s_done")
                if k >= 2:
                    P.w("tensor", c_s, f"uev_{k-2}")
            def mm_up(e, wpar=wpar, c=c, hf=hf):
                for r1 in range(4 * hf, 4 * hf + 4):
                    inst = e.matmul(pu[c % 2][:], wub[wpar][:, r1 * 128:(r1 + 1) * 128],
                                    xhat[:, r1 * 2048 + c * 512: r1 * 2048 + (c + 1) * 512],
                                    start=(r1 == 0), stop=(r1 == 15))
                return inst
            if hf == 3:
                P.do("tensor", mm_up, c_t, mark=f"upmm_{k}")
            else:
                P.do("tensor", mm_up)

        def tok_Fb(hi, c, hf):
            wpar = hi % 2
            k = hi * 4 + c
            if hf == 0 and k >= 1:
                P.w("tensor", c_s, f"gev_{k-1}")
            def mm_g(e, wpar=wpar, c=c, hf=hf):
                for r1 in range(4 * hf, 4 * hf + 4):
                    inst = e.matmul(pg[c % 2][:], wgb[wpar][:, r1 * 128:(r1 + 1) * 128],
                                    xhat[:, r1 * 2048 + c * 512: r1 * 2048 + (c + 1) * 512],
                                    start=(r1 == 0), stop=(r1 == 15))
                return inst
            if hf == 3:
                P.do("tensor", mm_g, c_t, mark=f"gmm_{k}")
            else:
                P.do("tensor", mm_g)

        def tok_F(hi, c):
            k = hi * 4 + c
            sl = slice(c * 512, (c + 1) * 512)
            P.w("scalar", c_t, f"upmm_{k}")
            if k >= 4:
                P.w("scalar", c_v, f"vtt_{k-4}")
            P.do("scalar", lambda e, hi=hi, sl=sl, c=c: e.activation(
                out=ub[:, sl], in_=pu[c % 2][:], func=AF.Copy, scale=wsu_eff[:, hi:hi + 1]),
                c_s, mark=f"uev_{k}")
            P.w("scalar", c_t, f"gmm_{k}")
            P.do("scalar", lambda e, hi=hi, sl=sl, c=c: e.activation(
                out=sgb[:, sl], in_=pg[c % 2][:], func=AF.Copy, scale=wsg_eff[:, hi:hi + 1]),
                c_s, mark=f"gev_{k}")
            if c == 3:
                P.do("scalar", lambda e, hi=hi: e.activation(
                    out=sgb[:], in_=sgb[:], func=AF.Silu), c_s, mark=f"slu_{hi}")

        def tok_Fpa(hi, hp):
            for c in (2 * hp, 2 * hp + 1):
                k = hi * 4 + c
                sl = slice(c * 512, (c + 1) * 512)
                P.w("vector", c_s, f"slu_{hi}")
                if hi >= 1 and c == 0:
                    P.w("vector", c_t, f"pamm_{(hi-1)*4+3}")
                    P.w("vector", c_vbst, f"vbst_{hi-1}")
                P.do("vector", lambda e, sl=sl: e.tensor_tensor(out=vtile[:, sl], in0=sgb[:, sl], in1=ub[:, sl], op=OP.mult),
                     c_v, mark=f"vtt_{k}")
                P.w("tensor", c_v, f"vtt_{k}")
                if k >= 1:
                    P.w("tensor", c_s, f"paev_{k-1}")
                P.do("tensor", lambda e, sl=sl: e.matmul(ppa3[:], h128b[:], vtile[:, sl], start=True, stop=True),
                     c_t, mark=f"pamm_{k}")
                if c == 3:
                    P.mk(f"fdone_{hi}", c_t)
                P.w("scalar", c_t, f"pamm_{k}")
                if hi >= 1 and c == 0:
                    P.w("scalar", c_vbst, f"vbst_{hi-1}")
                P.do("scalar", lambda e, sl=sl: e.copy(out=vA[:, sl], in_=ppa3[:]), c_s, mark=f"paev_{k}")

        def tok_Fsync(hi):
            P.w("sync", c_s, f"paev_{hi*4+3}")
            for b6 in range(2):
                P.dma("sync", lambda e, b6=b6, hi=hi: e.dma_start(
                    out=bass.AP(vb2.tensor, (b6 * 64 + hi) * 131072, [[2048, 64], [1, 2048]]),
                    in_=vA[b6 * 64:(b6 + 1) * 64, :]),
                    c_vbst, mark=f"vbst_{hi}" if b6 == 1 else None)

        def tok_Mdp(dc, c):
            if dc == 0 and c == 0:
                P.w("tensor", c_s, "p1c_s_done")
                P.w("tensor", c_v, "xdbl_v")
            if c == 0:
                if dc >= 2:
                    P.w("gpsimd", c_v, f"sev_{(dc-2)*4+3}")
                P.w("gpsimd", c_xnst, f"p1b_xnst_{dc}")
                P.dma("gpsimd", lambda e, dc=dc: e.dma_start(
                    out=xnf2[dc % 2][:], in_=xn_hbm[dc * 128:(dc + 1) * 128, :]), c_xnld, mark=f"xnld_{dc}")
            if dc >= 1:
                P.w("tensor", c_s, f"dA_{64*dc - 2 + (c % 2)}")
            if dc * 4 + c >= 2:
                P.w("tensor", c_s, f"softp_{dc*4+c-2}")
            P.do("tensor", lambda e, dc=dc, c=c: e.matmul(
                prepd[c % 2][:], wdtb[:, dc * 128:(dc + 1) * 128],
                dpreb[:, c * 512:(c + 1) * 512], start=True, stop=True), c_t, mark=f"dproj_{dc*4+c}")
            P.w("scalar", c_t, f"dproj_{dc*4+c}")
            if dc >= 2 and c == 0:
                P.w("scalar", c_t, f"repd_{64*(dc-1)-1}")
                P.w("scalar", c_v, f"ndx_{dc-2}")
            P.do("scalar", lambda e, c=c, dc=dc: e.activation(
                out=sigt[:], in_=prepd[c % 2][:], func=AF.Exp,
                bias=bdtn[:, dc:dc + 1]), c_s)
            P.do("scalar", lambda e, c=c, dc=dc: e.activation(
                out=nlsp[dc % 2][:, c * 512:(c + 1) * 512], in_=sigt[:], func=AF.Ln, bias=1.0),
                c_s, mark=f"softp_{dc*4+c}")

        def tok_Msp(dc):
            P.w("vector", c_s, f"softp_{dc*4+3}")
            P.w("vector", c_xnld, f"xnld_{dc}")
            if dc >= 1:
                P.w("vector", c_t, f"repx_{64*dc-1}")
            P.do("vector", lambda e, dc=dc: e.tensor_tensor(
                out=ndx[dc % 2][:], in0=nlsp[dc % 2][:], in1=xnf2[dc % 2][:], op=OP.mult),
                c_v, mark=f"ndx_{dc}")

        def tok_Mtile(dc, half, j):
            ti = (dc * 2 + half) * 16 + j
            par = ti % 3
            for cc in range(2):
                g = ti * 2 + cc
                if g % 64 < 2:
                    P.w("tensor", c_s, f"softp_{dc*4+3}")
                if g >= 2:
                    P.w("tensor", c_s, f"dA_{g-2}")
                P.do("tensor", lambda e, j=j, half=half, cc=cc, g=g, dc=dc: e.matmul(
                    prepd[g % 2][:], rpb[:, j * 128:(j + 1) * 128],
                    nlsp[dc % 2][:, half * 1024 + cc * 512: half * 1024 + (cc + 1) * 512],
                    start=True, stop=True), c_t, mark=f"repd_{g}")
                if g % 64 == 0:
                    P.w("tensor", c_v, f"ndx_{dc}")
                if g % 32 == 1:
                    key = f"sev_{dc*4 + half*2 - 1}" if (dc, half) != (0, 0) else None
                    if key:
                        P.w("tensor", c_v, key)
                if g >= 2:
                    P.w("tensor", c_v, f"BX_{g-2}")
                P.do("tensor", lambda e, j=j, half=half, cc=cc, g=g, dc=dc: e.matmul(
                    prepx[g % 2][:], rpb[:, j * 128:(j + 1) * 128],
                    ndx[dc % 2][:, half * 1024 + cc * 512: half * 1024 + (cc + 1) * 512],
                    start=True, stop=True), c_t, mark=f"repx_{g}")
                P.w("scalar", c_t, f"repd_{g}")
                if ti >= 3 and cc == 0:
                    P.w("scalar", c_v, f"scan_{ti-3}")
                P.do("scalar", lambda e, par=par, cc=cc, dc=dc, j=j, g=g: e.activation(
                    out=dAh[par][:, cc * 512:(cc + 1) * 512], in_=prepd[g % 2][:], func=AF.Exp,
                    scale=posA[:, dc * 16 + j: dc * 16 + j + 1]), c_s, mark=f"dA_{g}")
                P.w("vector", c_t, f"repx_{g}")
                P.do("vector", lambda e, par=par, half=half, cc=cc, g=g: e.tensor_tensor(
                    out=BXh[par][:, cc * 512:(cc + 1) * 512], in0=prepx[g % 2][:],
                    in1=B_rep[:, half * 1024 + cc * 512: half * 1024 + (cc + 1) * 512], op=OP.mult),
                    c_v, mark=f"BX_{g}")
            if DEBUG_TAPS and ti == 0:
                P.w("sync", c_s, "dA_1")
                P.dma("sync", lambda e: e.dma_start(out=dbg_dA[:], in_=dAh[0][:]), c_dbg)
                P.w("sync", c_v, "BX_1")
                P.dma("sync", lambda e: e.dma_start(out=dbg_BX[:], in_=BXh[0][:]), c_dbg, mark="dbg_t0")
            P.w("vector", c_s, f"dA_{ti*2+1}")
            hi_half = dc * 2 + half
            if hi_half >= 1:
                P.w("vector", c_t, f"yfree_{hi_half-1}", opt=True)
            def scan_fn(e, par=par, half=half, j=j):
                init = 0.0 if half == 0 else stcol[:, j:j + 1]
                return e.tensor_tensor_scan(out=hCs[j][:], data0=dAh[par][:], data1=BXh[par][:],
                                            initial=init, op0=OP.mult, op1=OP.add)
            P.do("vector", scan_fn, c_v, mark=f"scan_{ti}")
            if DEBUG_TAPS and ti == 0:
                P.w("sync", c_v, "scan_0")
                P.dma("sync", lambda e: e.dma_start(out=dbg_h[:], in_=hCs[0][:]), c_dbg, mark="dbg_h0")
                P.w("vector", c_dbg, "dbg_h0")
            if DEBUG_TAPS and ti == 2:
                P.w("vector", c_dbg, "dbg_t0")
                P.w("scalar", c_dbg, "dbg_t0")
            if half == 0:
                P.do("vector", lambda e, j=j: e.tensor_copy(stcol[:, j:j + 1], hCs[j][:, 1023:1024]), c_v)
            P.do("vector", lambda e, half=half, j=j: e.tensor_tensor(
                out=hCs[j][:], in0=hCs[j][:], in1=C_rep[:, half * 1024:(half + 1) * 1024], op=OP.mult),
                c_v, mark=f"hC_{ti}")

        def tok_My(dc, half):
            hi_half = dc * 2 + half
            for q in range(2):
                qq = half * 2 + q
                kq = dc * 4 + qq
                if kq >= 1:
                    P.w("tensor", c_v, f"sev_{kq-1}", opt=True)
                if q == 0:
                    P.w("tensor", c_v, f"hC_{hi_half*16+15}")
                def mm_y(e, hi_half=hi_half, q=q):
                    for j in range(16):
                        inst = e.matmul(py[:], sjb[:, j * 128:(j + 1) * 128],
                                        hCs[j][:, q * 512:(q + 1) * 512],
                                        start=(j == 0), stop=(j == 15))
                    return inst
                P.do("tensor", mm_y, c_t, mark=f"ymm_{kq}")
                if q == 1:
                    P.mk(f"yfree_{hi_half}", c_t)
                P.w("vector", c_t, f"ymm_{kq}")
                if kq >= 1:
                    P.w("vector", c_sst, f"sst_{kq-1}", opt=True)
                P.do("vector", lambda e, dc=dc, qq=qq: e.scalar_tensor_tensor(
                    out=sst[qq % 2][:], in0=xnf2[dc % 2][:, qq * 512:(qq + 1) * 512],
                    scalar=dcolv[:, dc:dc + 1], in1=py[:], op0=OP.mult, op1=OP.add),
                    c_v, mark=f"sev_{kq}")
                P.w("sync", c_v, f"sev_{kq}")
                P.dma("sync", lambda e, dc=dc, qq=qq: e.dma_start(
                    out=bass.AP(s_hbm.tensor, dc * 262144 + qq * 512, [[2048, 128], [1, 512]]),
                    in_=sst[qq % 2][:]), c_sst, mark=f"sst_{kq}")

        def interleave(a, b):
            out = []
            na, nb = len(a), len(b)
            wa = na * 1.14
            ia = ib = 0
            for k in range(na + nb):
                if ia < na and ia * nb <= ib * wa:
                    out.append(a[ia]); ia += 1
                elif ib < nb:
                    out.append(b[ib]); ib += 1
                else:
                    out.append(a[ia]); ia += 1
            return out

        for dc in range(NDC):
            mix = []
            if dc == 0:
                mix += [lambda c=c: tok_Mdp(0, c) for c in range(4)]
                mix.append(lambda: tok_Msp(0))
            for half in range(2):
                for j in range(16):
                    mix.append(lambda dc=dc, half=half, j=j: tok_Mtile(dc, half, j))
                if half == 1 and dc + 1 < NDC:
                    mix += [lambda dc=dc, c=c: tok_Mdp(dc + 1, c) for c in range(4)]
                    mix.append(lambda dc=dc: tok_Msp(dc + 1))
                mix.append(lambda dc=dc, half=half: tok_My(dc, half))
            ffn = []
            for k in range(4):
                hi = dc * 4 + k
                ffn.append(lambda hi=hi: tok_Fw(hi))
                for c in range(4):
                    for hf in range(4):
                        ffn.append(lambda hi=hi, c=c, hf=hf: tok_Fa(hi, c, hf))
                    for hf in range(4):
                        ffn.append(lambda hi=hi, c=c, hf=hf: tok_Fb(hi, c, hf))
                    ffn.append(lambda hi=hi, c=c: tok_F(hi, c))
                    if c == 3:
                        ffn.append(lambda hi=hi: tok_Fpa(hi, 0))
                        ffn.append(lambda hi=hi: tok_Fpa(hi, 1))
                ffn.append(lambda hi=hi: tok_Fsync(hi))
            for tok in interleave(mix, ffn):
                tok()
        P.mk("p2_t_done", c_t)
        P.mk("p2_s_done", c_s)
        P.mk("p2_v_done", c_v)

    def build_p3(bufs):
        (vhat, vbst_t, wdb, gt, xt3, st3, ot3, ppb, pdn) = bufs

        def emit_pb_chunk(tb, r2):
            kk = tb * 64 + r2
            if r2 >= 2:
                P.w("sync", c_t, f"pbmm_{kk-2}")
            P.dma("sync", lambda e, r2=r2, tb=tb: e.dma_start(
                out=vbst_t[r2 % 2][:], in_=bass.AP(vb2.tensor, r2 * 2048 + tb * 1024, [[131072, 128], [1, 1024]])),
                c_vbld, mark=f"vbld_{kk}")
            P.w("tensor", c_vbld, f"vbld_{kk}")
            if kk >= 2:
                P.w("tensor", c_s, f"pbev_{kk-2}")
            def mm_pb(e, r2=r2):
                for half in range(2):
                    inst = e.matmul(ppb[r2 % 2][:, half * 512:(half + 1) * 512], s2b[:],
                                    vbst_t[r2 % 2][:, half * 512:(half + 1) * 512],
                                    start=True, stop=True)
                return inst
            P.do("tensor", mm_pb, c_t, mark=f"pbmm_{kk}")
            P.w("scalar", c_t, f"pbmm_{kk}")
            P.do("scalar", lambda e, r2=r2: e.copy(
                out=vhat[:, r2 * 1024:(r2 + 1) * 1024], in_=ppb[r2 % 2][:]), c_s, mark=f"pbev_{kk}")

        def emit_dn_grp(tb, hc, grp, chase):
            k = tb * 16 + hc
            kg = k * 4 + grp
            if kg >= 2:
                P.w("gpsimd", c_t, f"dng_{kg-2}")
            P.dma("gpsimd", lambda e, hc=hc, grp=grp: e.dma_start(
                out=wdb[grp % 2][:], in_=bass.AP(wd_d.tensor, hc * 1048576 + grp * 2048,
                                                 [[8192, 128], [1, 2048]])),
                c_wd, mark=f"wdld_{kg}")
            if chase:
                P.w("tensor", c_s, f"pbev_{tb*64 + 16*grp + 15}")
            if grp == 0 and k >= 2:
                P.w("tensor", c_s, f"sil_{k-2}")
            P.w("tensor", c_wd, f"wdld_{kg}")
            def mm_dn(e, hc=hc, grp=grp):
                for rr in range(16):
                    r2 = grp * 16 + rr
                    for half in range(2):
                        inst = e.matmul(pdn[hc % 2][:, half * 512:(half + 1) * 512],
                                        wdb[grp % 2][:, rr * 128:(rr + 1) * 128],
                                        vhat[:, r2 * 1024 + half * 512: r2 * 1024 + (half + 1) * 512],
                                        start=(r2 == 0), stop=(r2 == 63))
                return inst
            P.do("tensor", mm_dn, c_t, mark=f"dng_{kg}")
            if grp == 3:
                P.mk(f"dnmm_{k}", c_t)

        def emit_dn_epi(tb, hc):
            k = tb * 16 + hc
            if k >= 2:
                P.w("sync", c_v, f"comb_{k-2}")
            P.dma("sync", lambda e, hc=hc, tb=tb: e.dma_start(
                out=xt3[hc % 2][:], in_=bass.AP(xT.tensor, hc * 262144 + tb * 1024, [[2048, 128], [1, 1024]])), c_xs)
            P.dma("sync", lambda e, hc=hc, tb=tb: e.dma_start(
                out=st3[hc % 2][:], in_=bass.AP(s_hbm.tensor, hc * 262144 + tb * 1024, [[2048, 128], [1, 1024]])),
                c_xs, mark=f"xsld_{k}")
            P.w("scalar", c_t, f"dnmm_{k}")
            if k >= 2:
                P.w("scalar", c_v, f"comb_{k-2}")
            P.do("scalar", lambda e, hc=hc, k=k: e.activation(
                out=gt[k % 2][:], in_=pdn[hc % 2][:], func=AF.Silu, scale=wsdp_eff[:, hc:hc + 1]),
                c_s, mark=f"sil_{k}")
            P.w("vector", c_s, f"sil_{k}")
            P.w("vector", c_xs, f"xsld_{k}")
            if k >= 2:
                P.w("vector", c_ost, f"ost_{k-2}")
            P.do("vector", lambda e, hc=hc, k=k: e.tensor_tensor(
                out=ot3[hc % 2][:], in0=st3[hc % 2][:], in1=gt[k % 2][:], op=OP.mult), c_v)
            P.do("vector", lambda e, hc=hc: e.tensor_tensor(
                out=ot3[hc % 2][:], in0=ot3[hc % 2][:], in1=xt3[hc % 2][:], op=OP.add),
                c_v, mark=f"comb_{k}")
            P.w("sync", c_v, f"comb_{k}")
            P.dma("sync", lambda e, hc=hc, tb=tb: e.dma_start(
                out=bass.AP(out_d.tensor, hc * 262144 + tb * 1024, [[2048, 128], [1, 1024]]),
                in_=ot3[hc % 2][:]), c_ost, mark=f"ost_{tb*16+hc}")

        for tb in range(2):
            if tb == 0:
                P.w("sync", c_vbst, f"vbst_{NI-1}")
                P.w("sync", c_sst, "sst_63")
                P.w("tensor", c_v, "p2_v_done")
                P.w("tensor", c_s, "p2_s_done")
            else:
                P.w("sync", c_t, f"dnmm_{(tb-1)*16+15}")
            for grp in range(4):
                for r2 in range(16 * grp, 16 * grp + 16):
                    emit_pb_chunk(tb, r2)
                emit_dn_grp(tb, 0, grp, chase=True)
            emit_dn_epi(tb, 0)
            for hc in range(1, 16):
                for grp in range(4):
                    emit_dn_grp(tb, hc, grp, chase=False)
                emit_dn_epi(tb, hc)
        P.w("sync", c_ost, val=16 * 32)

    p0()
    p1a()
    ps1.close()
    ps1b = contextlib.ExitStack()
    prep1 = ps1b.enter_context(nc.psum_tensor("prep1a", [128, 512], F32))
    pw1 = ps1b.enter_context(nc.psum_tensor("pw1", [96, 1], F32))
    p1a_epi(prep1, pw1)
    ps1b.close()
    ps1c = contextlib.ExitStack()
    ps_xd = ps1c.enter_context(nc.psum_tensor("ps_xd", [96, 2048], F32))
    ppa = [ps1c.enter_context(nc.psum_tensor(f"ppa{i}", [128, 512], F32)) for i in range(2)]
    p1b(ps_xd, ppa)
    p1b_epi(ps_xd, ppa)
    ps1c.close()
    ps1d = contextlib.ExitStack()
    pxh = [ps1d.enter_context(nc.psum_tensor(f"pxh{i}", [128, 512], F32)) for i in range(2)]
    p1c(pxh)
    ps1d.close()

    p1sc.close()
    for eng in ("sync", "gpsimd"):
        P.w(eng, c_t, "p1c_t_done")
        P.w(eng, c_s, "p1c_s_done")
        P.w(eng, c_v, "xdbl_v")
        P.w(eng, c_xnst, val=16 * NDC)
        P.w(eng, c_xbst, val=16 * 8 * NDC)

    p2sc = contextlib.ExitStack()
    sb2 = lambda n, s, dt: p2sc.enter_context(nc.sbuf_tensor(n, s, dt))
    wub = [sb2(f"wub{i}", [128, 2048], BF16) for i in range(2)]
    wgb = [sb2(f"wgb{i}", [128, 2048], BF16) for i in range(2)]
    ub = sb2("ub", [128, 2048], BF16)
    sgb = sb2("sgb", [128, 2048], BF16)
    vtile = sb2("vtile", [128, 2048], BF16)
    vA = vtile
    xnf2 = [sb2(f"xnf2_{i}", [128, 2048], BF16) for i in range(2)]
    nlsp = [sb2(f"nlsp{i}", [128, 2048], BF16) for i in range(2)]
    _nx = sb2("ndx0", [128, 2048], BF16)
    ndx = [_nx, _nx]
    dAh = [sb2(f"dAh{i}", [128, 1024], F32) for i in range(3)]
    _bx = sb2("BXh0", [128, 1024], BF16)
    BXh = [_bx, _bx, _bx]
    hCs = [sb2(f"hCs{i}", [128, 1024], BF16) for i in range(16)]
    stcol = sb2("stcol", [128, 16], F32)
    sigt = sb2("sigt", [128, 512], F32)
    _ss = sb2("sst0", [128, 512], F32)
    sst = [_ss, _ss]
    ps2 = contextlib.ExitStack()
    pu = [ps2.enter_context(nc.psum_tensor(f"pu{i}", [128, 512], F32)) for i in range(2)]
    pg0 = ps2.enter_context(nc.psum_tensor("pg0", [128, 512], F32))
    pg = [pg0, pg0]
    ppa2 = ps2.enter_context(nc.psum_tensor("ppa2", [128, 512], F32))
    prep = [ps2.enter_context(nc.psum_tensor(f"prep{i}", [128, 512], F32)) for i in range(2)]
    prep3 = ps2.enter_context(nc.psum_tensor("prep3", [128, 512], F32))
    py = ps2.enter_context(nc.psum_tensor("py", [128, 512], F32))
    build_p2((wub, wgb, ub, sgb, vtile, vA, xnf2, nlsp, ndx, dAh, BXh,
              hCs, stcol, sigt, sst, pu, pg, ppa2, prep, prep3, py))
    ps2.close()

    p2sc.close()
    xhsc.close()
    p12.close()
    for eng in ("sync", "gpsimd"):
        P.w(eng, c_t, "p2_t_done")
        P.w(eng, c_s, "p2_s_done")
        P.w(eng, c_v, "p2_v_done")
        P.w(eng, c_sst, val=16 * 64)
        P.w(eng, c_vbst, val=16 * 2 * NI)

    p3sc = contextlib.ExitStack()
    sb3 = lambda n, s, dt: p3sc.enter_context(nc.sbuf_tensor(n, s, dt))
    vhat = sb3("vhat", [128, 64 * 1024], BF16)
    vbst_t = [sb3(f"vbst{i}", [128, 1024], BF16) for i in range(2)]
    wdb = [sb3(f"wdb{i}", [128, 2048], BF16) for i in range(2)]
    gt = [sb3(f"gt{i}", [128, 1024], F32) for i in range(2)]
    xt3 = [sb3(f"xt3_{i}", [128, 1024], F32) for i in range(2)]
    st3 = [sb3(f"st3_{i}", [128, 1024], F32) for i in range(2)]
    ot3 = [sb3(f"ot3_{i}", [128, 1024], F32) for i in range(2)]
    ps3 = contextlib.ExitStack()
    ppb = [ps3.enter_context(nc.psum_tensor(f"ppb{i}", [128, 1024], F32)) for i in range(2)]
    pdn = [ps3.enter_context(nc.psum_tensor(f"pdn{i}", [128, 1024], F32)) for i in range(2)]
    build_p3((vhat, vbst_t, wdb, gt, xt3, st3, ot3, ppb, pdn))
    ps3.close()
    p3sc.close()

    with nc.Block() as blk:
        blk.sync(lambda e: [op(e) for op in P.ops["sync"]])
        blk.gpsimd(lambda e: [op(e) for op in P.ops["gpsimd"]])
        blk.tensor(lambda e: [op(e) for op in P.ops["tensor"]])
        blk.scalar(lambda e: [op(e) for op in P.ops["scalar"]])
        blk.vector(lambda e: [op(e) for op in P.ops["vector"]])
    es.close()
    build_program._last_P = P
    return nc


def kernel(**inputs):
    lay = host_layouts(inputs)
    cst = host_consts()
    x = np.asarray(inputs["L0_attn_x"], np.float32)
    nc = build_program()
    base = {}
    base.update(lay)
    base.update(cst)
    in_maps = []
    for b in range(8):
        m = dict(base)
        m["xT"] = np.ascontiguousarray(x[b].T)
        in_maps.append(m)
    res = run_bass_kernel_spmd(nc, in_maps, core_ids=list(range(8)))
    kernel._last_res = res
    out = np.empty((8, L, D), np.float32)
    for b in range(8):
        out[b] = res.results[b]["out"].T
    return out
